# version 18
# baseline (speedup 1.0000x reference)
import math
import numpy as np
import ml_dtypes
import concourse.bass as bass
import concourse.mybir as mybir
from concourse.bass_utils import run_bass_kernel_spmd

F32 = mybir.dt.float32
BF16 = mybir.dt.bfloat16
AF = mybir.ActivationFunctionType
ALU = mybir.AluOpType
AXX = mybir.AxisListType.X

NCORES = 8
S = 8192
D = 4096
NH = 32
NKV = 8
HD = 128
DKV = NKV * HD
E = 8
RLO = 16
TPC = S // NCORES
NTT = TPC // 128
NC = D // 128
SCALE = 1.0 / math.sqrt(HD)
EPS = 1e-5
MOE_SCALING = 2.0
KVROWS = 2 * TPC + 1
SKIP_CC = False


class Res:

    __slots__ = ("name", "writers", "readers")

    def __init__(self, name=""):
        self.name = name
        self.writers = {}
        self.readers = {}


class Op:
    __slots__ = ("idx", "eng", "fn", "dma", "deps", "needed", "semval",
                 "semkey", "kw")

    def __init__(self, idx, eng, fn, dma):
        self.idx = idx
        self.eng = eng
        self.fn = fn
        self.dma = dma
        self.deps = set()
        self.needed = False
        self.semval = None


class KB:

    ENGINES = ("sync", "scalar", "vector", "tensor", "gpsimd")

    def __init__(self, nc, ctx):
        self.nc = nc
        self.ctx = ctx
        self.ops = []
        self.phase_marks = [0]
        self.NSLOT = 8
        self.slot_ctr = {}
        self.sems = {}
        for e in self.ENGINES:
            self.sems[(e, "c")] = ctx.enter_context(nc.semaphore(f"sc_{e}"))
            for r in range(self.NSLOT):
                self.sems[(e, "q", r)] = ctx.enter_context(
                    nc.semaphore(f"sq_{e}{r}"))
        self.sems[("gpsimd", "cc")] = ctx.enter_context(nc.semaphore("s_cc"))

    def op(self, eng, fn, reads=(), writes=(), dma=False, cc=False):
        kind = "cc" if cc else ("q" if dma else "c")
        o = Op(len(self.ops), eng, fn, kind)
        if kind == "q":
            r_ = self.slot_ctr.get(eng, 0)
            self.slot_ctr[eng] = r_ + 1
            o.semkey = (eng, "q", r_ % self.NSLOT)
        else:
            o.semkey = (eng, kind)
        deps = set()
        for r in reads:
            deps.update(r.writers.values())
        for wres in writes:
            deps.update(wres.writers.values())
            deps.update(wres.readers.values())
        for d in list(deps):
            dop = self.ops[d]
            if (dop.eng == eng == "tensor" and dop.dma == "c"
                    and kind == "c"):
                deps.discard(d)
        o.deps = deps
        self.ops.append(o)
        for r in reads:
            prev = r.readers.get(o.semkey, -1)
            if o.idx > prev:
                r.readers[o.semkey] = o.idx
        for wres in writes:
            prev = wres.writers.get(o.semkey, -1)
            if o.idx > prev:
                wres.writers[o.semkey] = o.idx
            wres.readers = {}
        return o

    def dma(self, q, out, in_, reads=(), writes=(), **kw):
        return self.op(q, lambda e: e.dma_start(out=out, in_=in_, **kw),
                       reads, writes, dma=True)

    def phase(self):
        self.phase_marks.append(len(self.ops))

    def finalize(self):
        for o in self.ops:
            if o.dma != "c":
                o.needed = True
            for d in o.deps:
                self.ops[d].needed = True
        counts = {}
        for o in self.ops:
            if o.needed:
                counts[o.semkey] = counts.get(o.semkey, 0) + (
                    16 if o.dma == "q" else 1)
                o.semval = counts[o.semkey]
        marks = self.phase_marks + [len(self.ops)]
        for pi in range(len(marks) - 1):
            lo, hi = marks[pi], marks[pi + 1]
            if lo == hi:
                continue
            chunk = self.ops[lo:hi]
            with self.nc.Block() as blk:
                for eng_name in self.ENGINES:
                    my = [o for o in chunk if o.eng == eng_name]
                    if not my:
                        continue
                    self._emit_engine(blk, eng_name, my)

    def _emit_engine(self, blk, eng_name, my):
        ops_all = self.ops
        sems = self.sems
        waited = {}

        def body(eng):
            for o in my:
                if o.needed and o.dma == "q":
                    key = o.semkey
                    v = o.semval - 16
                    if v > 0 and waited.get(key, 0) < v:
                        eng.wait_ge(sems[key], v)
                        waited[key] = v
                for d in sorted(o.deps):
                    dop = ops_all[d]
                    key = dop.semkey
                    v = dop.semval
                    if waited.get(key, 0) < v:
                        eng.wait_ge(sems[key], v)
                        waited[key] = v
                instr = o.fn(eng)
                if o.needed:
                    instr.then_inc(sems[o.semkey],
                                   16 if o.dma == "q" else 1)
        getattr(blk, eng_name)(body)


class Arena:

    def __init__(self, tensor, nbytes, resfn):
        self.t = tensor
        self.free = [(0, nbytes)]
        self.allocs = {}
        self.resfn = resfn
        self.retired = []

    def alloc(self, name, nbytes, align=64):
        for i, (off, ln) in enumerate(self.free):
            aoff = (off + align - 1) // align * align
            if aoff + nbytes <= off + ln:
                pre = (off, aoff - off)
                post = (aoff + nbytes, off + ln - (aoff + nbytes))
                repl = [seg for seg in (pre, post) if seg[1] > 0]
                self.free[i:i + 1] = repl
                self.allocs[name] = (aoff, nbytes)
                r_new = self.resfn(name)
                keep = []
                for (o2, l2, r2) in self.retired:
                    if o2 < aoff + nbytes and aoff < o2 + l2:
                        for k, v in r2.readers.items():
                            if r_new.readers.get(k, -1) < v:
                                r_new.readers[k] = v
                        for k, v in r2.writers.items():
                            if r_new.writers.get(k, -1) < v:
                                r_new.writers[k] = v
                        if not (o2 >= aoff and o2 + l2 <= aoff + nbytes):
                            keep.append((o2, l2, r2))
                    else:
                        keep.append((o2, l2, r2))
                self.retired = keep
                return aoff
        raise MemoryError(f"sbuf arena: cannot alloc {name} ({nbytes} B); "
                          f"free={self.free}")

    def release(self, name):
        off, ln = self.allocs.pop(name)
        self.retired.append((off, ln, self.resfn(name)))
        self.free.append((off, ln))
        self.free.sort()
        merged = []
        for seg in self.free:
            if merged and merged[-1][0] + merged[-1][1] == seg[0]:
                merged[-1] = (merged[-1][0], merged[-1][1] + seg[1])
            else:
                merged.append(list(seg))
        self.free = [tuple(s) for s in merged]

    def bf(self, name, cols):
        off = self.alloc(name, cols * 2, align=64)
        assert off % 2 == 0
        return self.t[:, off // 2: off // 2 + cols]

    def f32(self, name, cols):
        off = self.alloc(name, cols * 4, align=64)
        return self.t[:, off // 2: off // 2 + cols * 2].bitcast(F32)


def build_nc(upto=6, dbg=False, dbg_dram=()):
    nc = bass.Bass("TRN2", target_bir_lowering=False, debug=False)

    dt_in = [
        ("x_own", [TPC, D], F32), ("w_q", [D, D], F32), ("w_k", [D, DKV], F32),
        ("w_v", [D, DKV], F32), ("w_o", [D, D], F32), ("w_base", [D, D], F32),
        ("b_base", [1, D], F32), ("w_rt", [E, D], F32),
        ("lora_a", [E * RLO, D], F32), ("lora_b", [E, D, RLO], F32),
        ("id_bf", [128, 128], BF16), ("id_f32", [128, 128], F32),
        ("dmask", [128, 1024], BF16), ("ones_col", [128, 1], BF16),
        ("ones_row", [1, 512], BF16), ("ones_rf", [1, 128], F32),
        ("e8", [E, 128], BF16),
    ]
    I = {}
    for nm, shp, dt in dt_in:
        I[nm] = nc.dram_tensor(nm, shp, dt, kind="ExternalInput").ap()
    out = nc.dram_tensor("out", [TPC, D], F32, kind="ExternalOutput").ap()

    dram = {}
    for nm, shp, dt, kw in [
        ("xbf", [TPC, D], BF16, {}), ("qbf", [D, TPC], BF16, {}),
        ("kvin", [KVROWS, DKV], BF16, {}),
        ("kvag", [NCORES * KVROWS, DKV], BF16, {"addr_space": "Shared"}),
        ("wbase_bf", [D, D], BF16, {}), ("bstk_d", [E * RLO, D], BF16, {}),
        ("q2d", [NH, TPC], F32, {}),
        ("ztm", [TPC, D], F32, {}), ("y1f", [TPC, D], F32, {}),
        ("y1bf", [TPC, D], BF16, {}), ("fftm", [TPC, D], F32, {}),
    ]:
        dram[nm] = nc.dram_tensor(nm, shp, dt, kind="Internal", **kw).ap()
    dbg_out = {}
    if dbg:
        dbg_out["misc"] = nc.dram_tensor("dbg_misc", [128, 1024], F32,
                                         kind="ExternalOutput").ap()
    for nm in dbg_dram:
        t = dram[nm].tensor
        dbg_out[nm] = nc.dram_tensor("dbgd_" + nm, list(t.shape), t.dtype,
                                     kind="ExternalOutput").ap()
    if dbg:
        dbg_out["attn"] = nc.dram_tensor("dbg_attn", [128, NH * TPC], BF16,
                                         kind="ExternalOutput").ap()

    from contextlib import ExitStack
    with ExitStack() as ctx:
        en = ctx.enter_context
        kb = KB(nc, ctx)

        idb = en(nc.sbuf_tensor("idb", [128, 128], BF16))
        idf = en(nc.sbuf_tensor("idf", [128, 128], F32))
        maskb = en(nc.sbuf_tensor("maskb", [128, 1024], BF16))
        onec = en(nc.sbuf_tensor("onec", [128, 1], BF16))
        oner = en(nc.sbuf_tensor("oner", [1, 512], BF16))
        onerf = en(nc.sbuf_tensor("onerf", [1, 128], F32))
        e8b = en(nc.sbuf_tensor("e8b", [E, 128], BF16))
        brow = en(nc.sbuf_tensor("brow", [1, D], BF16))
        astk = en(nc.sbuf_tensor("astk", [128, D], BF16))
        wrtT = en(nc.sbuf_tensor("wrtT", [128, NC * E], BF16))
        q2st = en(nc.sbuf_tensor("q2st", [128, NH * NTT], F32))
        biasall = en(nc.sbuf_tensor("biasall", [128, NH * NTT], F32))
        epsv = en(nc.sbuf_tensor("epsv", [128, 1], F32))

        AR_COLS = 90 * 1024
        art = en(nc.sbuf_tensor("arena", [128, AR_COLS], BF16))

        PD4 = [en(nc.psum_tensor(f"pd{i}", [128, 1024], F32)) for i in range(4)]
        PS = [PD4[i // 2][:, (i % 2) * 512:(i % 2) * 512 + 512]
              for i in range(8)]
        PD = [PD4[0], PD4[1]]

        def psb(i):
            return PS[i].bitcast(BF16)

        R = {}

        def res(name):
            if name not in R:
                R[name] = Res(name)
            return R[name]

        A = Arena(art, AR_COLS * 2, res)

        def link(dsts, srcs):
            for dn in dsts:
                rd = res(dn)
                for sn in srcs:
                    rs_ = res(sn)
                    for k, v in rs_.readers.items():
                        if rd.readers.get(k, -1) < v:
                            rd.readers[k] = v
                    for k, v in rs_.writers.items():
                        if rd.writers.get(k, -1) < v:
                            rd.writers[k] = v

        p0_la = A.bf("p0_la", D)
        p0_lb = [A.bf(f"p0_lb{i}", 512) for i in range(2)]
        p0_wr = A.bf("p0_wr", D)[0:8, :]
        p0_t16 = A.bf("p0_t16", D)[0:16, :]
        p0_id8 = A.bf("p0_id8", 8)[0:8, :]

        for nm, dst in (("id_bf", idb), ("id_f32", idf), ("dmask", maskb),
                        ("ones_col", onec), ("ones_row", oner),
                        ("ones_rf", onerf), ("e8", e8b)):
            kb.dma("sync", dst[:], I[nm], writes=[res(nm)])
        kb.dma("gpsimd", brow[:], I["b_base"], writes=[res("brow")])
        kb.dma("gpsimd", p0_la[:], I["lora_a"], writes=[res("p0_la")])
        kb.dma("gpsimd", p0_wr[:], I["w_rt"], writes=[res("p0_wr")])
        kb.dma("gpsimd", dram["xbf"], I["x_own"], writes=[res("xbf")])

        kb.op("vector", lambda v: v.tensor_copy(p0_id8[:], idb[0:8, 0:8]),
              reads=[res("id_bf")], writes=[res("p0_id8")])
        kb.op("vector", lambda v: v.memset(epsv[:], EPS), writes=[res("epsv")])

        for rnd in range(2):
            for cc in range(16):
                c = rnd * 16 + cc
                kb.op("tensor", (lambda c=c, cc=cc: lambda t: t.matmul(
                    PS[0][:, cc * 8:(cc + 1) * 8],
                    p0_wr[0:8, c * 128:(c + 1) * 128], p0_id8[:],
                    start=True, stop=True))(),
                    reads=[res("p0_wr"), res("p0_id8")], writes=[res("ps0")])
            kb.op("vector", (lambda rnd=rnd: lambda v: v.tensor_copy(
                wrtT[:, rnd * 128:(rnd + 1) * 128], PS[0][:, 0:128]))(),
                reads=[res("ps0")], writes=[res("wrtT")])

        for bt in range(4):
            bank = 1 + bt % 2
            for k in range(8):
                c = bt * 8 + k
                kb.op("tensor", (lambda c=c, k=k, bank=bank: lambda t: t.transpose(
                    psb(bank)[:, k * 128:(k + 1) * 128],
                    p0_la[:, c * 128:(c + 1) * 128], idb[:]))(),
                    reads=[res("p0_la"), res("id_bf")], writes=[res(f"ps{bank}")])
            kb.op("vector", (lambda bt=bt, bank=bank: lambda v: v.tensor_copy(
                astk[:, bt * 1024:(bt + 1) * 1024], psb(bank)[:]))(),
                reads=[res(f"ps{bank}")], writes=[res("astk")])

        lbv = I["lora_b"].rearrange("e (c p) r -> e p c r", p=128)
        for e_ix in range(E):
            dstb = p0_lb[e_ix % 2]
            dv = dstb.rearrange("p (c r) -> p c r", r=RLO)
            kb.dma("gpsimd", dv, lbv[e_ix], writes=[res(f"p0_lb{e_ix % 2}")])
            for grp in range(4):
                bank = 1 + (grp % 2)
                for k in range(8):
                    c = grp * 8 + k
                    kb.op("tensor", (lambda c=c, k=k, bank=bank, dv=dv:
                                     lambda t: t.transpose(
                        psb(bank)[0:16, k * 128:(k + 1) * 128],
                        dv[:, c, :], idb[:]))(),
                        reads=[res(f"p0_lb{e_ix % 2}"), res("id_bf")],
                        writes=[res(f"ps{bank}")])
                kb.op("vector", (lambda grp=grp, bank=bank: lambda v: v.tensor_copy(
                    p0_t16[:, grp * 1024:(grp + 1) * 1024],
                    psb(bank)[0:16, 0:1024]))(),
                    reads=[res(f"ps{bank}")], writes=[res("p0_t16")])
            kb.dma("sync", dram["bstk_d"][16 * e_ix:16 * (e_ix + 1), :],
                   p0_t16[:], reads=[res("p0_t16")], writes=[res("bstk_d")])
        for nm in ("p0_la", "p0_lb0", "p0_lb1", "p0_wr", "p0_t16", "p0_id8"):
            A.release(nm)
        xfm = A.bf("xfm", NC * TPC)
        wc = [A.bf(f"wc{i}", D) for i in range(3)]
        osl = [A.bf(f"osl{i}", TPC) for i in range(2)]
        vts = A.bf("vts", TPC)
        sqb = A.bf("sqb", TPC)
        q2r = A.f32("q2r", TPC)[0:1, :]
        k2l = A.f32("k2l", 64)[0:1, 0:1]
        k2l2 = A.f32("k2l2", 64)[0:1, 0:2]
        k2lb = A.bf("k2lb", 64)[0:1, 0:8]

        for c in range(NC):
            for th in range(2):
                kb.dma("sync",
                       xfm[:, c * TPC + th * 512: c * TPC + th * 512 + 512],
                       dram["xbf"][th * 512:(th + 1) * 512,
                                   c * 128:(c + 1) * 128],
                       reads=[res("xbf")], writes=[res(f"xfm{c}")],
                       transpose=True)

        wb_slice = [0]

        def emit_wbase_slice():
            i = wb_slice[0]
            if i >= 32:
                return
            wb_slice[0] += 1
            kb.dma("gpsimd", dram["wbase_bf"][i * 128:(i + 1) * 128, :],
                   I["w_base"][i * 128:(i + 1) * 128, :],
                   writes=[res(f"wbase_bf{i // 4}")])

        def load_colblock(wsrc, j, bi):
            wv = wsrc[:, j * 128:(j + 1) * 128].rearrange("(c p) o -> p c o", p=128)
            bv = wc[bi].rearrange("p (c o) -> p c o", o=128)
            for gix in range(4):
                kb.dma("gpsimd", bv[:, gix * 8:(gix + 1) * 8, :],
                       wv[:, gix * 8:(gix + 1) * 8, :], writes=[res(f"wc{bi}")])
            emit_wbase_slice()

        def proj_mms(bi):
            for half in range(2):
                for c in range(NC):
                    kb.op("tensor", (lambda bi=bi, c=c, half=half: lambda t:
                          t.matmul(PS[half][:],
                                   wc[bi][:, c * 128:(c + 1) * 128],
                                   xfm[:, c * TPC + half * 512:
                                       c * TPC + half * 512 + 512],
                                   start=(c == 0), stop=(c == NC - 1)))(),
                        reads=[res(f"wc{bi}"), res(f"xfm{c}")],
                        writes=[res(f"ps{half}")])

        def evac_slab(bi):
            for half in range(2):
                kb.op("vector", (lambda bi=bi, half=half: lambda v:
                      v.tensor_copy(osl[bi][:, half * 512:half * 512 + 512],
                                    PS[half][:]))(),
                    reads=[res(f"ps{half}")], writes=[res(f"osl{bi}")])

        def sq_ones(slab_res, h_or_j, kind):
            kb.op("scalar", (lambda bi=h_or_j % 2: lambda sc: sc.activation(
                sqb[:], osl[bi][:], AF.Square))(),
                reads=[res(slab_res)], writes=[res("sqb")])
            for half in range(2):
                kb.op("tensor", (lambda half=half: lambda t: t.matmul(
                    PS[2 + half][0:1, :],
                    onec[:], sqb[:, half * 512:half * 512 + 512],
                    start=True, stop=True))(),
                    reads=[res("sqb"), res("ones_col")],
                    writes=[res(f"ps{2 + half}")])

        p1_pend = [None]

        def p1_flush():
            if p1_pend[0] is not None:
                fn = p1_pend[0]
                p1_pend[0] = None
                fn()

        for j in range(NKV):
            bi = j % 3
            sl = j % 2
            load_colblock(I["w_k"], j, bi)
            proj_mms(bi)
            evac_slab(sl)

            def post_k(j=j, sl=sl):
                sq_ones(f"osl{sl}", j, "k")
                for half in range(2):
                    kb.op("vector", (lambda half=half: lambda v: v.reduce_max(
                        k2l2[0:1, half:half + 1], PS[2 + half][0:1, :], AXX))(),
                        reads=[res(f"ps{2 + half}")], writes=[res("k2l2")])
                kb.op("vector", lambda v: v.reduce_max(k2l, k2l2[0:1, 0:2], AXX),
                      reads=[res("k2l2")], writes=[res("k2l")])
                kb.op("vector", (lambda j=j: lambda v: v.tensor_copy(
                    k2lb[0:1, j:j + 1], k2l))(),
                    reads=[res("k2l")], writes=[res("k2lb")])
                kb.dma("sync", dram["kvin"][j * 128:(j + 1) * 128, :],
                       osl[sl][:],
                       reads=[res(f"osl{sl}")], writes=[res("kvin")])
            p1_flush()
            p1_pend[0] = post_k

        for j in range(NKV):
            bi = j % 3
            sl = j % 2
            load_colblock(I["w_v"], j, bi)
            proj_mms(bi)
            evac_slab(sl)

            def post_v(j=j, sl=sl):
                for tt in range(NTT):
                    kb.op("tensor", (lambda sl=sl, tt=tt: lambda t: t.transpose(
                        psb(4)[:, tt * 128:(tt + 1) * 128],
                        osl[sl][:, tt * 128:(tt + 1) * 128], idb[:]))(),
                        reads=[res(f"osl{sl}"), res("id_bf")],
                        writes=[res("ps4")])
                kb.op("vector", lambda v: v.tensor_copy(vts[:], psb(4)[:]),
                      reads=[res("ps4")], writes=[res("vts")])
                kv_dst = dram["kvin"][TPC:2 * TPC,
                                      j * 128:(j + 1) * 128].rearrange(
                    "(tt p) d -> p tt d", p=128)
                kb.dma("sync", kv_dst,
                       vts.rearrange("p (tt d) -> p tt d", d=128),
                       reads=[res("vts")], writes=[res("kvin")])
            p1_flush()
            p1_pend[0] = post_v

        p1_flush()
        kb.dma("sync", dram["kvin"][2 * TPC:2 * TPC + 1, 0:8], k2lb,
               reads=[res("k2lb")], writes=[res("kvin")])

        if not SKIP_CC:
            kb.op("gpsimd", lambda g: g.collective_compute(
                "AllGather", ALU.bypass, replica_groups=[list(range(NCORES))],
                ins=[dram["kvin"].opt()], outs=[dram["kvag"].opt()]),
                reads=[res("kvin")], writes=[res("kvag")], cc=True)

        for h in range(NH):
            bi = h % 3
            sl = h % 2
            load_colblock(I["w_q"], h, bi)
            proj_mms(bi)
            evac_slab(sl)

            def post_q(h=h, sl=sl):
                sq_ones(f"osl{sl}", h, "q")
                for half in range(2):
                    kb.op("vector", (lambda half=half: lambda v: v.tensor_copy(
                        q2r[0:1, half * 512:half * 512 + 512],
                        PS[2 + half][0:1, :]))(),
                        reads=[res(f"ps{2 + half}")], writes=[res("q2r")])
                for tt in range(NTT):
                    kb.op("tensor", (lambda tt=tt: lambda t: t.transpose(
                        PS[5][:, tt:tt + 1], q2r[0:1, tt * 128:(tt + 1) * 128],
                        idf[0:1, 0:1]))(),
                        reads=[res("q2r"), res("id_f32")], writes=[res("ps5")])
                kb.op("vector", (lambda h=h: lambda v: v.tensor_copy(
                    q2st[:, h * NTT:(h + 1) * NTT], PS[5][:, 0:NTT]))(),
                    reads=[res("ps5")], writes=[res("q2st")])
                kb.dma("sync", dram["qbf"][h * 128:(h + 1) * 128, :],
                       osl[sl][:],
                       reads=[res(f"osl{sl}")], writes=[res("qbf")])
            p1_flush()
            p1_pend[0] = post_q
        p1_flush()

        if upto < 2:
            _finish_dev(nc, kb, A, out, res, dram, dbg_out)
            kb.finalize()
            return nc, dbg_out

        link(["xfm"], [f"xfm{c}" for c in range(NC)])
        for nm in ("xfm", "wc0", "wc1", "wc2", "osl0", "osl1", "vts", "sqb",
                   "q2r", "k2l", "k2l2", "k2lb"):
            A.release(nm)
        attnfm = A.bf("attnfm", NH * TPC)
        kg = [A.bf(f"kg{i}", S) for i in range(2)]
        vg = [A.bf(f"vg{i}", S) for i in range(2)]
        qh = [A.bf(f"qh{i}", TPC) for i in range(2)]
        pbuf = [A.bf(f"pbuf{i}", 1024) for i in range(12)]
        ptmp = A.bf("ptmp", 1024)
        ptb = [A.bf(f"ptb{i}", 1024) for i in range(2)]
        lparts = A.f32("lparts", 4 * 32)
        lsum4 = A.f32("lsum4", 4)
        linv4 = A.f32("linv4", 4)
        lrow = A.f32("lrow", 512)[0:1, :]
        linvb = A.f32("linvb", 512)
        oraw = [A.f32(f"oraw{i}", 512) for i in range(2)]
        k2g8 = A.bf("k2g8", 64)[0:8, 0:8]
        k2m8 = A.f32("k2m8", 8)[0:8, 0:1]
        k2rowS = A.f32("k2rowS", 8)[0:1, :]
        k2mb = A.f32("k2mb", 8)
        tscr = A.f32("tscr", NH * NTT)
        tscr2 = A.f32("tscr2", NH * NTT)

        link(["pd0"], ["ps0", "ps1"])
        link(["pd1"], ["ps2", "ps3"])
        k2src = bass.AP(dram["kvag"].tensor, 2 * TPC * DKV,
                        [[1, 8], [KVROWS * DKV, 8]])
        kb.dma("sync", k2g8, k2src, reads=[res("kvag")], writes=[res("k2g8")],
               allow_slow_non_contiguous=True)
        kb.op("vector", lambda v: v.reduce_max(k2m8, k2g8, AXX),
              reads=[res("k2g8")], writes=[res("k2m8")])
        kb.op("tensor", lambda t: t.transpose(PS[6][0:1, 0:8], k2m8,
                                              idf[0:8, 0:8]),
              reads=[res("k2m8"), res("id_f32")], writes=[res("ps6")])
        kb.op("vector", lambda v: v.tensor_copy(k2rowS, PS[6][0:1, 0:8]),
              reads=[res("ps6")], writes=[res("k2rowS")])
        for g in range(8):
            kb.op("tensor", (lambda g=g: lambda t: t.matmul(
                PS[6][:, g:g + 1], onerf[:], k2rowS[0:1, g:g + 1],
                start=True, stop=True))(),
                reads=[res("k2rowS"), res("ones_rf")], writes=[res("ps6")])
        kb.op("vector", lambda v: v.tensor_copy(k2mb[:], PS[6][:, 0:8]),
              reads=[res("ps6")], writes=[res("k2mb")])
        for g in range(8):
            sl = slice(g * 32, (g + 1) * 32)
            kb.op("vector", (lambda g=g, sl=sl: lambda v: v.tensor_scalar_mul(
                tscr[:, sl], q2st[:, sl], k2mb[:, g:g + 1]))(),
                reads=[res("q2st"), res("k2mb")], writes=[res("tscr")])
        kb.op("scalar", lambda sc: sc.activation(
            tscr2[:], tscr[:], AF.Ln, scale=SCALE * SCALE),
            reads=[res("tscr")], writes=[res("tscr2")])
        kb.op("scalar", lambda sc: sc.activation(
            tscr[:], tscr2[:], AF.Exp, scale=0.5),
            reads=[res("tscr2")], writes=[res("tscr")])
        kb.op("vector", lambda v: v.tensor_scalar(
            biasall[:], tscr[:], -1.0, -1.0, op0=ALU.mult, op1=ALU.add),
            reads=[res("tscr")], writes=[res("biasall")])

        def load_kv(g, par):
            kgv = kg[par].rearrange("p (j m) -> p j m", j=8)
            for j8 in range(8):
                src = bass.AP(dram["kvag"].tensor,
                              (j8 * KVROWS + g * 128) * DKV,
                              [[DKV, 128], [1, TPC]])
                kb.dma("scalar", kgv[:, j8, :], src,
                       reads=[res("kvag")], writes=[res(f"kg{par}")])
            for j8 in range(8):
                src = bass.AP(dram["kvag"].tensor,
                              ((j8 * KVROWS + TPC) * DKV) + g * 128,
                              [[DKV, 128], [128 * DKV, 8], [1, 128]])
                dstv = vg[par].rearrange("p (l j d) -> p l j d", l=8, j=8)
                kb.dma("scalar", dstv[:, :, j8, :], src,
                       reads=[res("kvag")], writes=[res(f"vg{par}")])

        load_kv(0, 0)
        nheads_bias = res("biasall")

        PTB_B = (4, 5)
        s_ctr = [0]
        pt_ctr = [0]

        for g in range(8):
            par = g % 2
            if g < 7:
                load_kv(g + 1, (g + 1) % 2)
            kgv = kg[par].rearrange("p (j m) -> p j m", j=8)
            for hl in range(4):
                h = 4 * g + hl
                qp = h % 2
                kb.dma("sync", qh[qp][:, :], dram["qbf"][h * 128:(h + 1) * 128, :],
                       reads=[res("qbf")], writes=[res(f"qh{qp}")])
                for m in range(2):
                    tt0 = 4 * m
                    tts = list(range(tt0, tt0 + 4))
                    first_pv = [True]
                    pend = [None]

                    def emit_A_tt(L, tt):
                        sp = s_ctr[0] % 2
                        s_ctr[0] += 1
                        for half in range(2):
                            kb.op("tensor", (lambda sp=sp, tt=tt, L=L,
                                             half=half, qp=qp, kgv=kgv:
                                  lambda t: t.matmul(
                                      PD[sp][:, half * 512:half * 512 + 512],
                                      qh[qp][:, tt * 128:(tt + 1) * 128],
                                      kgv[:, half * 4:half * 4 + 4,
                                          L * 128:(L + 1) * 128],
                                      start=True, stop=True))(),
                                reads=[res(f"qh{qp}"), res(f"kg{par}")],
                                writes=[res(f"pd{sp}")])
                        dl = (tt - tt0) * 3 + (L % 3)
                        bi = L
                        bcol = biasall[:, h * 8 + tt:h * 8 + tt + 1]
                        if L < tt:
                            kb.op("scalar", (lambda sp=sp, dl=dl, bcol=bcol,
                                             tt=tt, bi=bi:
                                  lambda sc: sc.activation(
                                      pbuf[dl][:],
                                      PD[sp][:], AF.Exp, bias=bcol,
                                      scale=SCALE,
                                      accum_out=lparts[:, (tt - tt0) * 32
                                                       + bi:
                                                       (tt - tt0) * 32
                                                       + bi + 1]))(),
                                reads=[res(f"pd{sp}"), nheads_bias],
                                writes=[res(f"pbuf{dl}"), res("lparts")])
                        else:
                            kb.op("scalar", (lambda sp=sp, bcol=bcol:
                                  lambda sc: sc.activation(
                                      ptmp[:], PD[sp][:], AF.Exp,
                                      bias=bcol, scale=SCALE))(),
                                reads=[res(f"pd{sp}"), nheads_bias],
                                writes=[res("ptmp")])
                            kb.op("vector", (lambda dl=dl, tt=tt, bi=bi:
                                  lambda v: v.scalar_tensor_tensor(
                                      pbuf[dl][:],
                                      ptmp[:], 1.0, maskb[:],
                                      op0=ALU.mult, op1=ALU.mult,
                                      accum_out=lparts[:,
                                          (tt - tt0) * 32 + bi:
                                          (tt - tt0) * 32 + bi + 1]))(),
                                reads=[res("ptmp"), res("dmask")],
                                writes=[res(f"pbuf{dl}"), res("lparts")])

                    def flush_pend():
                        if not pend[0]:
                            return
                        (ptp, tbp, pvs) = pend[0]
                        pend[0] = None
                        kb.op("vector", (lambda ptp=ptp, tbp=tbp: lambda v:
                              v.tensor_copy(ptb[tbp][:], psb(ptp)[:]))(),
                            reads=[res(f"ps{ptp}")],
                            writes=[res(f"ptb{tbp}")])
                        for (chi, jh, col0, ncols, st, pv_par) in pvs:
                            kb.op("tensor", (lambda chi=chi, tbp=tbp, jh=jh,
                                             col0=col0, ncols=ncols, st=st,
                                             pv_par=pv_par: lambda t: t.matmul(
                                PS[6][:, col0:col0 + ncols],
                                vg[pv_par][:, chi * 128:(chi + 1) * 128],
                                ptb[tbp][:, jh * 512:jh * 512 + ncols],
                                start=st, stop=False,
                                skip_group_check=True))(),
                                reads=[res(f"ptb{tbp}"), res(f"vg{pv_par}")],
                                writes=[res("ps6")])

                    def emit_B_j8p(L, j8p):
                        vtts = [tt for tt in tts if tt >= L]
                        if not vtts:
                            return
                        ncols = len(vtts) * 128
                        col0 = (vtts[0] - tt0) * 128
                        ptp = PTB_B[pt_ctr[0] % 2]
                        tbp = pt_ctr[0] % 2
                        pt_ctr[0] += 1
                        for jh in range(2):
                            j8 = 2 * j8p + jh
                            for ix, tt in enumerate(vtts):
                                dl = (tt - tt0) * 3 + (L % 3)
                                kb.op("tensor", (lambda ptp=ptp, ix=ix,
                                                 dl=dl, j8=j8, jh=jh:
                                      lambda t: t.transpose(
                                    psb(ptp)[:, jh * 512 + ix * 128:
                                             jh * 512 + (ix + 1) * 128],
                                    pbuf[dl][:, j8 * 128:(j8 + 1) * 128],
                                    idb[:]))(),
                                    reads=[res(f"pbuf{dl}"), res("id_bf")],
                                    writes=[res(f"ps{ptp}")])
                        pvs = []
                        for jh in range(2):
                            j8 = 2 * j8p + jh
                            chi = L * 8 + j8
                            st = first_pv[0]
                            first_pv[0] = False
                            pvs.append((chi, jh, col0, ncols, st, par))
                        prev = pend[0]
                        pend[0] = (ptp, tbp, pvs)
                        if prev is not None:
                            tmp = pend[0]
                            pend[0] = prev
                            flush_pend()
                            pend[0] = tmp

                    Lmax = tt0 + 3
                    for tt in tts:
                        emit_A_tt(0, tt)
                    for tt in tts:
                        if tt >= 1:
                            emit_A_tt(1, tt)
                    for L in range(2, Lmax + 1):
                        amini = [tt for tt in tts if tt >= L]
                        for i in range(4):
                            if i < len(amini):
                                emit_A_tt(L, amini[i])
                            emit_B_j8p(L - 2, i)
                    emit_B_j8p(Lmax - 1, 0)
                    emit_B_j8p(Lmax - 1, 1)
                    emit_B_j8p(Lmax - 1, 2)
                    emit_B_j8p(Lmax - 1, 3)
                    for i in range(4):
                        emit_B_j8p(Lmax, i)
                    flush_pend()
                    orp = (h * 2 + m) % 2
                    kb.op("vector", (lambda orp=orp: lambda v: v.tensor_copy(
                        oraw[orp][:], PS[6][:]))(),
                        reads=[res("ps6")], writes=[res(f"oraw{orp}")])
                    for ix, tt in enumerate(tts):
                        kb.op("vector", (lambda ix=ix, nbi=tt + 1:
                              lambda v: v.reduce_sum(
                                  lsum4[:, ix:ix + 1],
                                  lparts[:, ix * 32:ix * 32 + nbi],
                                  AXX))(),
                            reads=[res("lparts")], writes=[res("lsum4")])
                    kb.op("vector", lambda v: v.reciprocal(
                        linv4[:], lsum4[:]),
                        reads=[res("lsum4")], writes=[res("linv4")])
                    for ix in range(4):
                        kb.op("tensor", (lambda ix=ix: lambda t: t.transpose(
                            PS[7][0:1, ix * 128:(ix + 1) * 128],
                            linv4[:, ix:ix + 1], idf[:]))(),
                            reads=[res("linv4"), res("id_f32")],
                            writes=[res("ps7")])
                    kb.op("vector", lambda v: v.tensor_copy(lrow, PS[7][0:1, :]),
                          reads=[res("ps7")], writes=[res("lrow")])
                    kb.op("tensor", lambda t: t.matmul(
                        PS[7][:], onerf[:], lrow, start=True, stop=True),
                        reads=[res("lrow"), res("ones_rf")], writes=[res("ps7")])
                    kb.op("vector", lambda v: v.tensor_copy(linvb[:], PS[7][:]),
                          reads=[res("ps7")], writes=[res("linvb")])
                    acol = h * TPC + m * 512
                    kb.op("vector", (lambda acol=acol, orp=orp: lambda v:
                          v.tensor_tensor(attnfm[:, acol:acol + 512],
                                          oraw[orp][:], linvb[:],
                                          op=ALU.mult))(),
                        reads=[res(f"oraw{orp}"), res("linvb")],
                        writes=[res("attnfm")])
        if dbg:
            for hc in range(NH):
                kb.dma("sync", dbg_out["attn"][:, hc * TPC:(hc + 1) * TPC],
                       attnfm[:, hc * TPC:(hc + 1) * TPC],
                       reads=[res("attnfm")], writes=[res("outw")])
            kb.dma("sync", dbg_out["misc"][:, 0:256], biasall[:],
                   reads=[res("biasall")], writes=[res("outw")])
            kb.dma("sync", dbg_out["misc"][:, 256:512], q2st[:],
                   reads=[res("q2st")], writes=[res("outw")])
            kb.dma("sync", dbg_out["misc"][:, 512:520], k2mb[:],
                   reads=[res("k2mb")], writes=[res("outw")])
            kb.dma("sync", dbg_out["misc"][0:8, 520:524],
                   k2g8.bitcast(F32)[0:8, 0:4],
                   reads=[res("k2g8")], writes=[res("outw")])

        if upto < 3:
            _finish_dev(nc, kb, A, out, res, dram, dbg_out)
            kb.finalize()
            return nc, dbg_out

        for nm in ("kg0", "kg1", "vg0", "vg1", "qh0", "qh1", "ptmp",
                   "ptb0", "ptb1", "lparts", "lsum4", "linv4", "lrow", "linvb",
                   "oraw0", "oraw1",
                   "k2g8", "k2m8", "k2rowS", "k2mb", "tscr", "tscr2"):
            A.release(nm)
        for i in range(12):
            A.release(f"pbuf{i}")

        link(["ps0", "ps1"], ["pd0"])
        link(["ps2", "ps3"], ["pd1"])
        woc = [A.bf(f"woc{i}", D) for i in range(2)]
        tmpo = [A.f32(f"tmpo{i}", TPC) for i in range(2)]
        tmpz = [A.f32(f"tmpz{i}", TPC) for i in range(2)]

        def fm_to_tm_phase(rhs_sb, rhs_res, lhs_loader, dst_dram, dst_res,
                           extra_mms=None):
            def emit_tail(oc):
                zi = oc % 2
                for tt in range(NTT):
                    pb_ = 2 + tt % 2
                    kb.op("tensor", (lambda pb_=pb_, tt=tt, zi=zi: lambda t:
                          t.transpose(PS[pb_][:, 0:128],
                                      tmpo[zi][:, tt * 128:(tt + 1) * 128],
                                      idf[:]))(),
                        reads=[res(f"tmpo{zi}"), res("id_f32")],
                        writes=[res(f"ps{pb_}")])
                    kb.op("scalar", (lambda pb_=pb_, tt=tt, zi=zi: lambda sc:
                          sc.copy(tmpz[zi][:, tt * 128:(tt + 1) * 128],
                                  PS[pb_][:, 0:128]))(),
                        reads=[res(f"ps{pb_}")], writes=[res(f"tmpz{zi}")])
                dstv = dst_dram.rearrange("(tt p) d -> p tt d", p=128)
                for th in range(2):
                    kb.dma("sync", dstv[:, th * 4:(th + 1) * 4,
                                        oc * 128:(oc + 1) * 128],
                           tmpz[zi][:, th * 512:(th + 1) * 512].rearrange(
                               "p (tt o) -> p tt o", o=128),
                           reads=[res(f"tmpz{zi}")],
                           writes=[res(dst_res + str(th))])

            for oc in range(NC):
                bi = oc % 2
                lhs_loader(oc, bi)
                for half in range(2):
                    for c in range(NC):
                        kb.op("tensor", (lambda bi=bi, c=c, half=half:
                              lambda t: t.matmul(
                                  PS[half][:],
                                  woc[bi][:, c * 128:(c + 1) * 128],
                                  rhs_sb[:, c * TPC + half * 512:
                                         c * TPC + half * 512 + 512],
                                  start=(c == 0),
                                  stop=(c == NC - 1 and extra_mms is None)))(),
                            reads=[res(f"woc{bi}"), rhs_res],
                            writes=[res(f"ps{half}")])
                    if extra_mms is not None:
                        extra_mms(oc, half)
                    zi0 = oc % 2
                    kb.op("vector", (lambda half=half, zi=zi0: lambda v:
                          v.tensor_copy(tmpo[zi][:, half * 512:half * 512 + 512],
                                        PS[half][:]))(),
                        reads=[res(f"ps{half}")], writes=[res(f"tmpo{zi0}")])
                if oc > 0:
                    emit_tail(oc - 1)
            emit_tail(NC - 1)

        def wo_loader(oc, bi):
            wv = I["w_o"][:, oc * 128:(oc + 1) * 128].rearrange(
                "(c p) o -> p c o", p=128)
            bv = woc[bi].rearrange("p (c o) -> p c o", o=128)
            for gix in range(4):
                kb.dma("gpsimd", bv[:, gix * 8:(gix + 1) * 8, :],
                       wv[:, gix * 8:(gix + 1) * 8, :], writes=[res(f"woc{bi}")])

        fm_to_tm_phase(attnfm, res("attnfm"), wo_loader, dram["ztm"], "ztm")

        if upto < 4:
            _finish_dev(nc, kb, A, out, res, dram, dbg_out)
            kb.finalize()
            return nc, dbg_out

        A.release("attnfm")
        A.release("woc0")
        A.release("woc1")
        A.release("tmpo0")
        A.release("tmpo1")
        A.release("tmpz0")
        A.release("tmpz1")

        def ln_phase(src_dram, src_res_pfx, res_dram, store_fn):
            bufa = [A.f32(f"lnA{i}", D) for i in range(2)]
            bufb = [A.f32(f"lnB{i}", D) for i in range(2)]
            bufz = [A.f32(f"lnZ{i}", D) for i in range(2)]
            svecs = [A.f32(f"lnS{i}", 16) for i in range(2)]
            for tt in range(NTT):
                p_ = tt % 2
                svec = svecs[p_]
                rz = res(f"lnZ{p_}")
                ra = res(f"lnA{p_}")
                rb = res(f"lnB{p_}")
                rs = res(f"lnS{p_}")
                kb.dma("sync", bufa[p_][:], src_dram[tt * 128:(tt + 1) * 128, :],
                       reads=[res(src_res_pfx + str(tt // 4))], writes=[ra])
                kb.dma("sync", bufb[p_][:],
                       res_dram[tt * 128:(tt + 1) * 128, :],
                       reads=[res("y1f")] if res_dram is dram["y1f"] else [],
                       writes=[rb])
                kb.op("vector", (lambda p_=p_: lambda v: v.tensor_add(
                    bufz[p_][:], bufa[p_][:], bufb[p_][:]))(),
                    reads=[ra, rb], writes=[rz])
                kb.op("vector", (lambda p_=p_: lambda v: v.reduce_sum(
                    svec[:, 0:1], bufz[p_][:], AXX))(),
                    reads=[rz], writes=[rs])
                kb.op("scalar", (lambda p_=p_: lambda sc: sc.activation(
                    bufa[p_][:], bufz[p_][:], AF.Square))(),
                    reads=[rz], writes=[ra])
                kb.op("vector", (lambda p_=p_: lambda v: v.reduce_sum(
                    svec[:, 1:2], bufa[p_][:], AXX))(),
                    reads=[ra], writes=[rs])
                kb.op("vector", lambda v: v.tensor_scalar_mul(
                    svec[:, 2:3], svec[:, 0:1], 1.0 / D),
                    reads=[rs], writes=[rs])
                kb.op("scalar", lambda sc: sc.activation(
                    svec[:, 3:4], svec[:, 2:3], AF.Square),
                    reads=[rs], writes=[rs])
                kb.op("vector", lambda v: v.tensor_scalar(
                    svec[:, 4:5], svec[:, 1:2], 1.0 / D, None, op0=ALU.mult),
                    reads=[rs], writes=[rs])
                kb.op("vector", lambda v: v.tensor_sub(
                    svec[:, 5:6], svec[:, 4:5], svec[:, 3:4]),
                    reads=[rs], writes=[rs])
                kb.op("scalar", lambda sc: sc.activation(
                    svec[:, 6:7], svec[:, 5:6], AF.Ln, bias=epsv[:]),
                    reads=[rs, res("epsv")], writes=[rs])
                kb.op("scalar", lambda sc: sc.activation(
                    svec[:, 7:8], svec[:, 6:7], AF.Exp, scale=-0.5),
                    reads=[rs], writes=[rs])
                kb.op("vector", lambda v: v.tensor_mul(
                    svec[:, 8:9], svec[:, 2:3], svec[:, 7:8]),
                    reads=[rs], writes=[rs])
                kb.op("vector", (lambda p_=p_: lambda v: v.tensor_scalar(
                    bufb[p_][:], bufz[p_][:], svec[:, 2:3], svec[:, 7:8],
                    op0=ALU.subtract, op1=ALU.mult))(),
                    reads=[rz, rs], writes=[rb])
                store_fn(tt, bufb[p_], rb)
            for i in range(2):
                A.release(f"lnA{i}")
                A.release(f"lnB{i}")
                A.release(f"lnZ{i}")
                A.release(f"lnS{i}")

        def y1_store(tt, buf, rbuf):
            kb.dma("scalar", dram["y1f"][tt * 128:(tt + 1) * 128, :], buf[:],
                   reads=[rbuf], writes=[res("y1f")])
            kb.dma("gpsimd", dram["y1bf"][tt * 128:(tt + 1) * 128, :], buf[:],
                   reads=[rbuf], writes=[res(f"y1bf{tt // 4}")])

        ln_phase(dram["ztm"], "ztm", I["x_own"], y1_store)

        if upto < 5:
            _finish_dev(nc, kb, A, out, res, dram, dbg_out)
            kb.finalize()
            return nc, dbg_out

        yfm = A.bf("yfm", NC * TPC)
        wbt = [A.bf(f"wbt{i}", NC * 512) for i in range(2)]
        bstk = A.bf("bstk", D)
        logf = A.f32("logf", TPC)[0:8, :]
        ltm = A.f32("ltm", 8)
        e8t = A.f32("e8t", 8)
        cnt = A.f32("cnt", 8)
        tj8 = A.f32("tj8", 8)
        wmsk = A.f32("wmsk", 8)
        wt8 = A.f32("wt8", 8)
        sc8 = A.f32("sc8", 8)
        wfmb = A.bf("wfmb", TPC)[0:8, :]
        wbs = A.f32("wbs", TPC)
        hs = A.f32("hs", TPC)
        gsb = A.bf("gsb", TPC)
        tmpo5 = [A.f32(f"tmpo{i}", TPC) for i in range(2)]
        tmpz5 = [A.f32(f"tmpz{i}", TPC) for i in range(2)]

        kb.dma("sync", bstk[:], dram["bstk_d"],
               reads=[res("bstk_d")], writes=[res("bstk")])
        for c in range(NC):
            for th in range(2):
                kb.dma("sync",
                       yfm[:, c * TPC + th * 512: c * TPC + th * 512 + 512],
                       dram["y1bf"][th * 512:(th + 1) * 512,
                                    c * 128:(c + 1) * 128],
                       reads=[res(f"y1bf{th}")], writes=[res("yfm")],
                       transpose=True)

        for half in range(2):
            for c in range(NC):
                kb.op("tensor", (lambda c=c, half=half: lambda t: t.matmul(
                    PS[0][0:8, :], wrtT[:, c * 8:(c + 1) * 8],
                    yfm[:, c * TPC + half * 512: c * TPC + half * 512 + 512],
                    start=(c == 0), stop=(c == NC - 1)))(),
                    reads=[res("wrtT"), res("yfm")], writes=[res("ps0")])
            kb.op("vector", (lambda half=half: lambda v: v.tensor_copy(
                logf[:, half * 512:half * 512 + 512], PS[0][0:8, :]))(),
                reads=[res("ps0")], writes=[res("logf")])

        for tt in range(NTT):
            kb.op("tensor", (lambda tt=tt: lambda t: t.matmul(
                PS[1][:, 0:8], logf[0:8, tt * 128:(tt + 1) * 128],
                idf[0:8, 0:8], start=True, stop=True))(),
                reads=[res("logf"), res("id_f32")], writes=[res("ps1")])
            kb.op("vector", lambda v: v.tensor_copy(ltm[:], PS[1][:, 0:8]),
                  reads=[res("ps1")], writes=[res("ltm")])
            kb.op("vector", lambda v: v.reduce_max(sc8[:, 0:1], ltm[:], AXX),
                  reads=[res("ltm")], writes=[res("sc8")])
            kb.op("vector", lambda v: v.tensor_scalar_mul(
                sc8[:, 1:2], sc8[:, 0:1], -1.0),
                reads=[res("sc8")], writes=[res("sc8")])
            kb.op("scalar", lambda sc: sc.activation(
                e8t[:], ltm[:], AF.Exp, bias=sc8[:, 1:2]),
                reads=[res("ltm"), res("sc8")], writes=[res("e8t")])
            kb.op("vector", lambda v: v.memset(cnt[:], 0.0),
                  writes=[res("cnt")])
            for j in range(E):
                kb.op("vector", (lambda j=j: lambda v: v.tensor_single_scalar(
                    tj8[:], e8t[:], e8t[:, j:j + 1], op=ALU.is_lt))(),
                    reads=[res("e8t")], writes=[res("tj8")])
                kb.op("vector", lambda v: v.tensor_add(cnt[:], cnt[:], tj8[:]),
                      reads=[res("tj8"), res("cnt")], writes=[res("cnt")])
            kb.op("vector", lambda v: v.tensor_single_scalar(
                wmsk[:], cnt[:], 2.0, op=ALU.is_lt),
                reads=[res("cnt")], writes=[res("wmsk")])
            kb.op("vector", lambda v: v.tensor_mul(wt8[:], e8t[:], wmsk[:]),
                  reads=[res("e8t"), res("wmsk")], writes=[res("wt8")])
            kb.op("vector", lambda v: v.reduce_sum(sc8[:, 2:3], wt8[:], AXX),
                  reads=[res("wt8")], writes=[res("sc8")])
            kb.op("vector", lambda v: v.reciprocal(sc8[:, 3:4], sc8[:, 2:3]),
                  reads=[res("sc8")], writes=[res("sc8")])
            kb.op("vector", lambda v: v.tensor_scalar_mul(
                wt8[:], wt8[:], sc8[:, 3:4]),
                reads=[res("wt8"), res("sc8")], writes=[res("wt8")])
            kb.op("tensor", lambda t: t.transpose(PS[1][0:8, 0:128], wt8[:],
                                                  idf[:]),
                  reads=[res("wt8"), res("id_f32")], writes=[res("ps1")])
            kb.op("vector", (lambda tt=tt: lambda v: v.tensor_copy(
                wfmb[:, tt * 128:(tt + 1) * 128], PS[1][0:8, 0:128]))(),
                reads=[res("ps1")], writes=[res("wfmb")])

        for half in range(2):
            kb.op("tensor", (lambda half=half: lambda t: t.matmul(
                PS[2][:], e8b[:], wfmb[:, half * 512:half * 512 + 512],
                start=True, stop=True))(),
                reads=[res("wfmb"), res("e8")], writes=[res("ps2")])
            kb.op("vector", (lambda half=half: lambda v: v.tensor_copy(
                wbs[:, half * 512:half * 512 + 512], PS[2][:]))(),
                reads=[res("ps2")], writes=[res("wbs")])
            for c in range(NC):
                kb.op("tensor", (lambda c=c, half=half: lambda t: t.matmul(
                    PS[3][:], astk[:, c * 128:(c + 1) * 128],
                    yfm[:, c * TPC + half * 512: c * TPC + half * 512 + 512],
                    start=(c == 0), stop=(c == NC - 1)))(),
                    reads=[res("astk"), res("yfm")], writes=[res("ps3")])
            kb.op("vector", (lambda half=half: lambda v: v.tensor_copy(
                hs[:, half * 512:half * 512 + 512], PS[3][:]))(),
                reads=[res("ps3")], writes=[res("hs")])
        kb.op("vector", lambda v: v.tensor_mul(gsb[:], hs[:], wbs[:]),
              reads=[res("hs"), res("wbs")], writes=[res("gsb")])

        def load_wbt(ob):
            par = ob % 2
            for c in range(NC):
                kb.dma("sync", wbt[par][:, c * 512:(c + 1) * 512],
                       dram["wbase_bf"][ob * 512:(ob + 1) * 512,
                                        c * 128:(c + 1) * 128],
                       reads=[res(f"wbase_bf{ob}")],
                       writes=[res(f"wbt{par}")], transpose=True)

        load_wbt(0)
        load_wbt(1)

        def wb_loader(oc, bi):
            pass

        def wb_lhs(oc):
            ob = oc // 4
            par = ob % 2
            oc4 = oc % 4
            return wbt[par], par, oc4

        def p5_loader(oc, bi):
            wb_loader(oc, bi)

        def p5_extra(oc, half):
            kb.op("tensor", (lambda oc=oc, half=half: lambda t: t.matmul(
                PS[half][:], bstk[:, oc * 128:(oc + 1) * 128],
                gsb[:, half * 512:half * 512 + 512],
                start=False, stop=False, skip_group_check=True))(),
                reads=[res("bstk"), res("gsb")], writes=[res(f"ps{half}")])
            kb.op("tensor", (lambda oc=oc, half=half: lambda t: t.matmul(
                PS[half][:], brow[0:1, oc * 128:(oc + 1) * 128],
                oner[:], start=False, stop=True, skip_group_check=True))(),
                reads=[res("brow"), res("ones_row")], writes=[res(f"ps{half}")])

        def p5_tail(oc):
            zi = oc % 2
            for tt in range(NTT):
                pb_ = 2 + tt % 2
                kb.op("tensor", (lambda pb_=pb_, tt=tt, zi=zi: lambda t:
                      t.transpose(PS[pb_][:, 0:128],
                                  tmpo5[zi][:, tt * 128:(tt + 1) * 128],
                                  idf[:]))(),
                    reads=[res(f"tmpo{zi}"), res("id_f32")],
                    writes=[res(f"ps{pb_}")])
                kb.op("scalar", (lambda pb_=pb_, tt=tt, zi=zi: lambda sc:
                      sc.copy(tmpz5[zi][:, tt * 128:(tt + 1) * 128],
                              PS[pb_][:, 0:128]))(),
                    reads=[res(f"ps{pb_}")], writes=[res(f"tmpz{zi}")])
            dstv = dram["fftm"].rearrange("(tt p) d -> p tt d", p=128)
            for th in range(2):
                kb.dma("sync", dstv[:, th * 4:(th + 1) * 4,
                                    oc * 128:(oc + 1) * 128],
                       tmpz5[zi][:, th * 512:(th + 1) * 512].rearrange(
                           "p (tt o) -> p tt o", o=128),
                       reads=[res(f"tmpz{zi}")], writes=[res(f"fftm{th}")])

        for oc in range(NC):
            p5_loader(oc, oc % 2)
            slab, par, oc4 = wb_lhs(oc)
            for half in range(2):
                for c in range(NC):
                    kb.op("tensor", (lambda slab=slab, par=par, oc4=oc4, c=c,
                                     half=half: lambda t: t.matmul(
                        PS[half][:],
                        slab[:, c * 512 + oc4 * 128: c * 512 + oc4 * 128 + 128],
                        yfm[:, c * TPC + half * 512: c * TPC + half * 512 + 512],
                        start=(c == 0), stop=False,
                        skip_group_check=True))(),
                        reads=[res(f"wbt{par}"), res("yfm")],
                        writes=[res(f"ps{half}")])
                p5_extra(oc, half)
                zi0 = oc % 2
                kb.op("vector", (lambda half=half, zi=zi0: lambda v:
                      v.tensor_copy(tmpo5[zi][:, half * 512:half * 512 + 512],
                                    PS[half][:]))(),
                    reads=[res(f"ps{half}")], writes=[res(f"tmpo{zi0}")])
            if oc > 0:
                p5_tail(oc - 1)
            if oc % 4 == 3 and oc // 4 + 2 < 8:
                load_wbt(oc // 4 + 2)
        p5_tail(NC - 1)

        if upto < 6:
            _finish_dev(nc, kb, A, out, res, dram, dbg_out)
            kb.finalize()
            return nc, dbg_out

        for nm in ("yfm", "wbt0", "wbt1", "bstk", "logf", "ltm", "e8t", "cnt",
                   "tj8", "wmsk", "wt8", "sc8", "wfmb", "wbs", "hs", "gsb",
                   "tmpo0", "tmpo1", "tmpz0", "tmpz1"):
            A.release(nm)

        def out_store(tt, buf, rbuf):
            kb.dma("scalar", out[tt * 128:(tt + 1) * 128, :], buf[:],
                   reads=[rbuf], writes=[res("outw")])

        ln_phase(dram["fftm"], "fftm", dram["y1f"], out_store)
        kb.op("sync", lambda e: e.nop(), reads=[res("outw")])

        kb.finalize()
    return nc, dbg_out


def _finish_dev(nc, kb, A, out, res, dram=None, dbg_map=None):
    zz = A.f32("zz_dev", 1024)
    kb.op("gpsimd", lambda g: g.memset(zz[:], 0.0), writes=[res("zz_dev")])
    for tt in range(NTT):
        for q4 in range(4):
            kb.dma("gpsimd", out[tt * 128:(tt + 1) * 128,
                                 q4 * 1024:(q4 + 1) * 1024], zz[:],
                   reads=[res("zz_dev")], writes=[res("outw")])
    if dbg_map:
        for nm, dst in dbg_map.items():
            if nm not in dram:
                continue
            kb.dma("sync", dst, dram[nm], reads=[res(nm)], writes=[res("outw")])
    kb.op("gpsimd", lambda g: g.engine_nop(), reads=[res("outw")])


def _host_consts(core_i):
    m = np.zeros((128, 1024), dtype=np.float32)
    for j8 in range(8):
        k = 0 if j8 <= core_i else -1
        m[:, j8 * 128:(j8 + 1) * 128] = np.tril(np.ones((128, 128), np.float32), k)
    e8v = np.zeros((E, 128), dtype=np.float32)
    for e_ix in range(E):
        e8v[e_ix, e_ix * RLO:(e_ix + 1) * RLO] = MOE_SCALING
    return {
        "id_bf": np.eye(128, dtype=ml_dtypes.bfloat16),
        "id_f32": np.eye(128, dtype=np.float32),
        "dmask": m.astype(ml_dtypes.bfloat16),
        "ones_col": np.ones((128, 1), dtype=ml_dtypes.bfloat16),
        "ones_row": np.ones((1, 512), dtype=ml_dtypes.bfloat16),
        "ones_rf": np.ones((1, 128), dtype=np.float32),
        "e8": e8v.astype(ml_dtypes.bfloat16),
    }


def make_in_maps(x, wq, wk, wv, wo, wbase, bbase, wrt, la, lbv):
    shared = {
        "w_q": np.ascontiguousarray(wq, dtype=np.float32),
        "w_k": np.ascontiguousarray(wk, dtype=np.float32),
        "w_v": np.ascontiguousarray(wv, dtype=np.float32),
        "w_o": np.ascontiguousarray(wo, dtype=np.float32),
        "w_base": np.ascontiguousarray(wbase, dtype=np.float32),
        "b_base": np.ascontiguousarray(bbase, dtype=np.float32).reshape(1, D),
        "w_rt": np.ascontiguousarray(wrt, dtype=np.float32),
        "lora_a": np.ascontiguousarray(la, dtype=np.float32).reshape(E * RLO, D),
        "lora_b": np.ascontiguousarray(lbv, dtype=np.float32),
    }
    in_maps = []
    for i in range(NCORES):
        mm = dict(shared)
        mm["x_own"] = np.ascontiguousarray(x[i::8, :], dtype=np.float32)
        mm.update(_host_consts(i))
        in_maps.append(mm)
    return in_maps


def kernel(L0_attn_hidden_states, L0_attn_w_q, L0_attn_w_k, L0_attn_w_v,
           L0_attn_w_o, L0_ffn_W_base, L0_ffn_b_base, L0_ffn_W_router,
           L0_ffn_lora_A, L0_ffn_lora_B):
    x = np.asarray(L0_attn_hidden_states).reshape(S, D)
    nc, _ = build_nc()
    in_maps = make_in_maps(x, L0_attn_w_q, L0_attn_w_k, L0_attn_w_v,
                           L0_attn_w_o, L0_ffn_W_base, L0_ffn_b_base,
                           L0_ffn_W_router, L0_ffn_lora_A, L0_ffn_lora_B)
    res_ = run_bass_kernel_spmd(nc, in_maps, core_ids=list(range(NCORES)))
    full = np.empty((S, D), dtype=np.float32)
    for i in range(NCORES):
        full[i::8, :] = res_.results[i]["out"]
    return full.reshape(1, S, D)


# version 19
# speedup vs baseline: 1.1117x; 1.1117x over previous
import math
import numpy as np
import ml_dtypes
import concourse.bass as bass
import concourse.mybir as mybir
from concourse.bass_utils import run_bass_kernel_spmd

F32 = mybir.dt.float32
BF16 = mybir.dt.bfloat16
AF = mybir.ActivationFunctionType
ALU = mybir.AluOpType
AXX = mybir.AxisListType.X

NCORES = 8
S = 8192
D = 4096
NH = 32
NKV = 8
HD = 128
DKV = NKV * HD
E = 8
RLO = 16
TPC = S // NCORES
NTT = TPC // 128
NC = D // 128
SCALE = 1.0 / math.sqrt(HD)
EPS = 1e-5
MOE_SCALING = 2.0
KVROWS = 2 * TPC + 1
SKIP_CC = False


class Res:

    __slots__ = ("name", "writers", "readers")

    def __init__(self, name=""):
        self.name = name
        self.writers = {}
        self.readers = {}


class Op:
    __slots__ = ("idx", "eng", "fn", "dma", "deps", "needed", "semval",
                 "semkey", "kw")

    def __init__(self, idx, eng, fn, dma):
        self.idx = idx
        self.eng = eng
        self.fn = fn
        self.dma = dma
        self.deps = set()
        self.needed = False
        self.semval = None


class KB:

    ENGINES = ("sync", "scalar", "vector", "tensor", "gpsimd")

    def __init__(self, nc, ctx):
        self.nc = nc
        self.ctx = ctx
        self.ops = []
        self.phase_marks = [0]
        self.NSLOT = 8
        self.slot_ctr = {}
        self.sems = {}
        for e in self.ENGINES:
            self.sems[(e, "c")] = ctx.enter_context(nc.semaphore(f"sc_{e}"))
            for r in range(self.NSLOT):
                self.sems[(e, "q", r)] = ctx.enter_context(
                    nc.semaphore(f"sq_{e}{r}"))
        self.sems[("gpsimd", "cc")] = ctx.enter_context(nc.semaphore("s_cc"))

    def op(self, eng, fn, reads=(), writes=(), dma=False, cc=False):
        kind = "cc" if cc else ("q" if dma else "c")
        o = Op(len(self.ops), eng, fn, kind)
        if kind == "q":
            r_ = self.slot_ctr.get(eng, 0)
            self.slot_ctr[eng] = r_ + 1
            o.semkey = (eng, "q", r_ % self.NSLOT)
        else:
            o.semkey = (eng, kind)
        deps = set()
        for r in reads:
            deps.update(r.writers.values())
        for wres in writes:
            deps.update(wres.writers.values())
            deps.update(wres.readers.values())
        for d in list(deps):
            dop = self.ops[d]
            if (dop.eng == eng == "tensor" and dop.dma == "c"
                    and kind == "c"):
                deps.discard(d)
        o.deps = deps
        self.ops.append(o)
        for r in reads:
            prev = r.readers.get(o.semkey, -1)
            if o.idx > prev:
                r.readers[o.semkey] = o.idx
        for wres in writes:
            prev = wres.writers.get(o.semkey, -1)
            if o.idx > prev:
                wres.writers[o.semkey] = o.idx
            wres.readers = {}
        return o

    def dma(self, q, out, in_, reads=(), writes=(), **kw):
        return self.op(q, lambda e: e.dma_start(out=out, in_=in_, **kw),
                       reads, writes, dma=True)

    def phase(self):
        self.phase_marks.append(len(self.ops))

    def finalize(self):
        for o in self.ops:
            if o.dma != "c":
                o.needed = True
            for d in o.deps:
                self.ops[d].needed = True
        counts = {}
        for o in self.ops:
            if o.needed:
                counts[o.semkey] = counts.get(o.semkey, 0) + (
                    16 if o.dma == "q" else 1)
                o.semval = counts[o.semkey]
        marks = self.phase_marks + [len(self.ops)]
        for pi in range(len(marks) - 1):
            lo, hi = marks[pi], marks[pi + 1]
            if lo == hi:
                continue
            chunk = self.ops[lo:hi]
            with self.nc.Block() as blk:
                for eng_name in self.ENGINES:
                    my = [o for o in chunk if o.eng == eng_name]
                    if not my:
                        continue
                    self._emit_engine(blk, eng_name, my)

    def _emit_engine(self, blk, eng_name, my):
        ops_all = self.ops
        sems = self.sems
        waited = {}

        def body(eng):
            for o in my:
                if o.needed and o.dma == "q":
                    key = o.semkey
                    v = o.semval - 16
                    if v > 0 and waited.get(key, 0) < v:
                        eng.wait_ge(sems[key], v)
                        waited[key] = v
                for d in sorted(o.deps):
                    dop = ops_all[d]
                    key = dop.semkey
                    v = dop.semval
                    if waited.get(key, 0) < v:
                        eng.wait_ge(sems[key], v)
                        waited[key] = v
                instr = o.fn(eng)
                if o.needed:
                    instr.then_inc(sems[o.semkey],
                                   16 if o.dma == "q" else 1)
        getattr(blk, eng_name)(body)


class Arena:

    def __init__(self, tensor, nbytes, resfn):
        self.t = tensor
        self.free = [(0, nbytes)]
        self.allocs = {}
        self.resfn = resfn
        self.retired = []

    def alloc(self, name, nbytes, align=64):
        for i, (off, ln) in enumerate(self.free):
            aoff = (off + align - 1) // align * align
            if aoff + nbytes <= off + ln:
                pre = (off, aoff - off)
                post = (aoff + nbytes, off + ln - (aoff + nbytes))
                repl = [seg for seg in (pre, post) if seg[1] > 0]
                self.free[i:i + 1] = repl
                self.allocs[name] = (aoff, nbytes)
                r_new = self.resfn(name)
                keep = []
                for (o2, l2, r2) in self.retired:
                    if o2 < aoff + nbytes and aoff < o2 + l2:
                        for k, v in r2.readers.items():
                            if r_new.readers.get(k, -1) < v:
                                r_new.readers[k] = v
                        for k, v in r2.writers.items():
                            if r_new.writers.get(k, -1) < v:
                                r_new.writers[k] = v
                        if not (o2 >= aoff and o2 + l2 <= aoff + nbytes):
                            keep.append((o2, l2, r2))
                    else:
                        keep.append((o2, l2, r2))
                self.retired = keep
                return aoff
        raise MemoryError(f"sbuf arena: cannot alloc {name} ({nbytes} B); "
                          f"free={self.free}")

    def release(self, name):
        off, ln = self.allocs.pop(name)
        self.retired.append((off, ln, self.resfn(name)))
        self.free.append((off, ln))
        self.free.sort()
        merged = []
        for seg in self.free:
            if merged and merged[-1][0] + merged[-1][1] == seg[0]:
                merged[-1] = (merged[-1][0], merged[-1][1] + seg[1])
            else:
                merged.append(list(seg))
        self.free = [tuple(s) for s in merged]

    def bf(self, name, cols):
        off = self.alloc(name, cols * 2, align=64)
        assert off % 2 == 0
        return self.t[:, off // 2: off // 2 + cols]

    def f32(self, name, cols):
        off = self.alloc(name, cols * 4, align=64)
        return self.t[:, off // 2: off // 2 + cols * 2].bitcast(F32)


def build_nc(upto=6, dbg=False, dbg_dram=()):
    nc = bass.Bass("TRN2", target_bir_lowering=False, debug=False)

    dt_in = [
        ("x_own", [TPC, D], F32), ("w_q", [D, D], F32), ("w_k", [D, DKV], F32),
        ("w_v", [D, DKV], F32), ("w_o", [D, D], F32), ("w_base", [D, D], F32),
        ("b_base", [1, D], F32), ("w_rt", [E, D], F32),
        ("lora_a", [E * RLO, D], F32), ("lora_b", [E, D, RLO], F32),
        ("id_bf", [128, 128], BF16), ("id_f32", [128, 128], F32),
        ("dmask", [128, 1024], BF16), ("ones_col", [128, 1], BF16),
        ("ones_row", [1, 512], BF16), ("ones_rf", [1, 128], F32),
        ("e8", [E, 128], BF16),
    ]
    I = {}
    for nm, shp, dt in dt_in:
        I[nm] = nc.dram_tensor(nm, shp, dt, kind="ExternalInput").ap()
    out = nc.dram_tensor("out", [TPC, D], F32, kind="ExternalOutput").ap()

    dram = {}
    for nm, shp, dt, kw in [
        ("xbf", [TPC, D], BF16, {}), ("qbf", [D, TPC], BF16, {}),
        ("kvin", [KVROWS, DKV], BF16, {}),
        ("kvag", [NCORES * KVROWS, DKV], BF16, {"addr_space": "Shared"}),
        ("wbase_bf", [D, D], BF16, {}), ("bstk_d", [E * RLO, D], BF16, {}),
        ("q2d", [NH, TPC], F32, {}),
        ("ztm", [TPC, D], F32, {}), ("y1f", [TPC, D], F32, {}),
        ("y1bf", [TPC, D], BF16, {}), ("fftm", [TPC, D], F32, {}),
    ]:
        dram[nm] = nc.dram_tensor(nm, shp, dt, kind="Internal", **kw).ap()
    dbg_out = {}
    if dbg:
        dbg_out["misc"] = nc.dram_tensor("dbg_misc", [128, 1024], F32,
                                         kind="ExternalOutput").ap()
    for nm in dbg_dram:
        t = dram[nm].tensor
        dbg_out[nm] = nc.dram_tensor("dbgd_" + nm, list(t.shape), t.dtype,
                                     kind="ExternalOutput").ap()
    if dbg:
        dbg_out["attn"] = nc.dram_tensor("dbg_attn", [128, NH * TPC], BF16,
                                         kind="ExternalOutput").ap()

    from contextlib import ExitStack
    with ExitStack() as ctx:
        en = ctx.enter_context
        kb = KB(nc, ctx)

        idb = en(nc.sbuf_tensor("idb", [128, 128], BF16))
        idf = en(nc.sbuf_tensor("idf", [128, 128], F32))
        maskb = en(nc.sbuf_tensor("maskb", [128, 1024], BF16))
        onec = en(nc.sbuf_tensor("onec", [128, 1], BF16))
        oner = en(nc.sbuf_tensor("oner", [1, 512], BF16))
        onerf = en(nc.sbuf_tensor("onerf", [1, 128], F32))
        e8b = en(nc.sbuf_tensor("e8b", [E, 128], BF16))
        brow = en(nc.sbuf_tensor("brow", [1, D], BF16))
        astk = en(nc.sbuf_tensor("astk", [128, D], BF16))
        wrtT = en(nc.sbuf_tensor("wrtT", [128, NC * E], BF16))
        q2st = en(nc.sbuf_tensor("q2st", [128, NH * NTT], F32))
        biasall = en(nc.sbuf_tensor("biasall", [128, NH * NTT], F32))
        epsv = en(nc.sbuf_tensor("epsv", [128, 1], F32))

        AR_COLS = 90 * 1024
        art = en(nc.sbuf_tensor("arena", [128, AR_COLS], BF16))

        PD4 = [en(nc.psum_tensor(f"pd{i}", [128, 1024], F32)) for i in range(4)]
        PS = [PD4[i // 2][:, (i % 2) * 512:(i % 2) * 512 + 512]
              for i in range(8)]
        PD = [PD4[0], PD4[1]]

        def psb(i):
            return PS[i].bitcast(BF16)

        R = {}

        def res(name):
            if name not in R:
                R[name] = Res(name)
            return R[name]

        A = Arena(art, AR_COLS * 2, res)

        def link(dsts, srcs):
            for dn in dsts:
                rd = res(dn)
                for sn in srcs:
                    rs_ = res(sn)
                    for k, v in rs_.readers.items():
                        if rd.readers.get(k, -1) < v:
                            rd.readers[k] = v
                    for k, v in rs_.writers.items():
                        if rd.writers.get(k, -1) < v:
                            rd.writers[k] = v

        p0_la = A.bf("p0_la", D)
        p0_lb = [A.bf(f"p0_lb{i}", 512) for i in range(2)]
        p0_wr = A.bf("p0_wr", D)[0:8, :]
        p0_t16 = A.bf("p0_t16", D)[0:16, :]
        p0_id8 = A.bf("p0_id8", 8)[0:8, :]

        for nm, dst in (("id_bf", idb), ("id_f32", idf), ("dmask", maskb),
                        ("ones_col", onec), ("ones_row", oner),
                        ("ones_rf", onerf), ("e8", e8b)):
            kb.dma("sync", dst[:], I[nm], writes=[res(nm)])
        kb.dma("gpsimd", brow[:], I["b_base"], writes=[res("brow")])
        kb.dma("gpsimd", p0_la[:], I["lora_a"], writes=[res("p0_la")])
        kb.dma("gpsimd", p0_wr[:], I["w_rt"], writes=[res("p0_wr")])
        kb.dma("gpsimd", dram["xbf"], I["x_own"], writes=[res("xbf")])

        kb.op("vector", lambda v: v.tensor_copy(p0_id8[:], idb[0:8, 0:8]),
              reads=[res("id_bf")], writes=[res("p0_id8")])
        kb.op("vector", lambda v: v.memset(epsv[:], EPS), writes=[res("epsv")])

        for rnd in range(2):
            for cc in range(16):
                c = rnd * 16 + cc
                kb.op("tensor", (lambda c=c, cc=cc: lambda t: t.matmul(
                    PS[0][:, cc * 8:(cc + 1) * 8],
                    p0_wr[0:8, c * 128:(c + 1) * 128], p0_id8[:],
                    start=True, stop=True))(),
                    reads=[res("p0_wr"), res("p0_id8")], writes=[res("ps0")])
            kb.op("vector", (lambda rnd=rnd: lambda v: v.tensor_copy(
                wrtT[:, rnd * 128:(rnd + 1) * 128], PS[0][:, 0:128]))(),
                reads=[res("ps0")], writes=[res("wrtT")])

        for bt in range(4):
            bank = 1 + bt % 2
            for k in range(8):
                c = bt * 8 + k
                kb.op("tensor", (lambda c=c, k=k, bank=bank: lambda t: t.transpose(
                    psb(bank)[:, k * 128:(k + 1) * 128],
                    p0_la[:, c * 128:(c + 1) * 128], idb[:]))(),
                    reads=[res("p0_la"), res("id_bf")], writes=[res(f"ps{bank}")])
            kb.op("vector", (lambda bt=bt, bank=bank: lambda v: v.tensor_copy(
                astk[:, bt * 1024:(bt + 1) * 1024], psb(bank)[:]))(),
                reads=[res(f"ps{bank}")], writes=[res("astk")])

        lbv = I["lora_b"].rearrange("e (c p) r -> e p c r", p=128)
        for e_ix in range(E):
            dstb = p0_lb[e_ix % 2]
            dv = dstb.rearrange("p (c r) -> p c r", r=RLO)
            kb.dma("gpsimd", dv, lbv[e_ix], writes=[res(f"p0_lb{e_ix % 2}")])
            for grp in range(4):
                bank = 1 + (grp % 2)
                for k in range(8):
                    c = grp * 8 + k
                    kb.op("tensor", (lambda c=c, k=k, bank=bank, dv=dv:
                                     lambda t: t.transpose(
                        psb(bank)[0:16, k * 128:(k + 1) * 128],
                        dv[:, c, :], idb[:]))(),
                        reads=[res(f"p0_lb{e_ix % 2}"), res("id_bf")],
                        writes=[res(f"ps{bank}")])
                kb.op("vector", (lambda grp=grp, bank=bank: lambda v: v.tensor_copy(
                    p0_t16[:, grp * 1024:(grp + 1) * 1024],
                    psb(bank)[0:16, 0:1024]))(),
                    reads=[res(f"ps{bank}")], writes=[res("p0_t16")])
            kb.dma("sync", dram["bstk_d"][16 * e_ix:16 * (e_ix + 1), :],
                   p0_t16[:], reads=[res("p0_t16")], writes=[res("bstk_d")])
        for nm in ("p0_la", "p0_lb0", "p0_lb1", "p0_wr", "p0_t16", "p0_id8"):
            A.release(nm)
        xfm = A.bf("xfm", NC * TPC)
        wc = [A.bf(f"wc{i}", D) for i in range(3)]
        osl = [A.bf(f"osl{i}", TPC) for i in range(2)]
        vts = A.bf("vts", TPC)
        sqb = A.bf("sqb", TPC)
        q2r = A.f32("q2r", TPC)[0:1, :]
        k2l = A.f32("k2l", 64)[0:1, 0:1]
        k2l2 = A.f32("k2l2", 64)[0:1, 0:2]
        k2lb = A.bf("k2lb", 64)[0:1, 0:8]

        for c in range(NC):
            for th in range(2):
                kb.dma("sync",
                       xfm[:, c * TPC + th * 512: c * TPC + th * 512 + 512],
                       dram["xbf"][th * 512:(th + 1) * 512,
                                   c * 128:(c + 1) * 128],
                       reads=[res("xbf")], writes=[res(f"xfm{c}")],
                       transpose=True)

        wb_slice = [0]

        def emit_wbase_slice():
            i = wb_slice[0]
            if i >= 32:
                return
            wb_slice[0] += 1
            kb.dma("gpsimd", dram["wbase_bf"][i * 128:(i + 1) * 128, :],
                   I["w_base"][i * 128:(i + 1) * 128, :],
                   writes=[res(f"wbase_bf{i // 4}")])

        def load_colblock(wsrc, j, bi):
            wv = wsrc[:, j * 128:(j + 1) * 128].rearrange("(c p) o -> p c o", p=128)
            bv = wc[bi].rearrange("p (c o) -> p c o", o=128)
            for gix in range(4):
                kb.dma("gpsimd", bv[:, gix * 8:(gix + 1) * 8, :],
                       wv[:, gix * 8:(gix + 1) * 8, :], writes=[res(f"wc{bi}")])
            emit_wbase_slice()

        def proj_mms(bi):
            for half in range(2):
                for c in range(NC):
                    kb.op("tensor", (lambda bi=bi, c=c, half=half: lambda t:
                          t.matmul(PS[half][:],
                                   wc[bi][:, c * 128:(c + 1) * 128],
                                   xfm[:, c * TPC + half * 512:
                                       c * TPC + half * 512 + 512],
                                   start=(c == 0), stop=(c == NC - 1)))(),
                        reads=[res(f"wc{bi}"), res(f"xfm{c}")],
                        writes=[res(f"ps{half}")])

        def evac_slab(bi):
            for half in range(2):
                kb.op("vector", (lambda bi=bi, half=half: lambda v:
                      v.tensor_copy(osl[bi][:, half * 512:half * 512 + 512],
                                    PS[half][:]))(),
                    reads=[res(f"ps{half}")], writes=[res(f"osl{bi}")])

        def sq_ones(slab_res, h_or_j, kind):
            kb.op("scalar", (lambda bi=h_or_j % 2: lambda sc: sc.activation(
                sqb[:], osl[bi][:], AF.Square))(),
                reads=[res(slab_res)], writes=[res("sqb")])
            for half in range(2):
                kb.op("tensor", (lambda half=half: lambda t: t.matmul(
                    PS[2 + half][0:1, :],
                    onec[:], sqb[:, half * 512:half * 512 + 512],
                    start=True, stop=True))(),
                    reads=[res("sqb"), res("ones_col")],
                    writes=[res(f"ps{2 + half}")])

        p1_pend = [None]

        def p1_flush():
            if p1_pend[0] is not None:
                fn = p1_pend[0]
                p1_pend[0] = None
                fn()

        for j in range(NKV):
            bi = j % 3
            sl = j % 2
            load_colblock(I["w_k"], j, bi)
            proj_mms(bi)
            evac_slab(sl)

            def post_k(j=j, sl=sl):
                sq_ones(f"osl{sl}", j, "k")
                for half in range(2):
                    kb.op("vector", (lambda half=half: lambda v: v.reduce_max(
                        k2l2[0:1, half:half + 1], PS[2 + half][0:1, :], AXX))(),
                        reads=[res(f"ps{2 + half}")], writes=[res("k2l2")])
                kb.op("vector", lambda v: v.reduce_max(k2l, k2l2[0:1, 0:2], AXX),
                      reads=[res("k2l2")], writes=[res("k2l")])
                kb.op("vector", (lambda j=j: lambda v: v.tensor_copy(
                    k2lb[0:1, j:j + 1], k2l))(),
                    reads=[res("k2l")], writes=[res("k2lb")])
                kb.dma("sync", dram["kvin"][j * 128:(j + 1) * 128, :],
                       osl[sl][:],
                       reads=[res(f"osl{sl}")], writes=[res("kvin")])
            p1_flush()
            p1_pend[0] = post_k

        for j in range(NKV):
            bi = j % 3
            sl = j % 2
            load_colblock(I["w_v"], j, bi)
            proj_mms(bi)
            evac_slab(sl)

            def post_v(j=j, sl=sl):
                for tt in range(NTT):
                    kb.op("tensor", (lambda sl=sl, tt=tt: lambda t: t.transpose(
                        psb(4)[:, tt * 128:(tt + 1) * 128],
                        osl[sl][:, tt * 128:(tt + 1) * 128], idb[:]))(),
                        reads=[res(f"osl{sl}"), res("id_bf")],
                        writes=[res("ps4")])
                kb.op("vector", lambda v: v.tensor_copy(vts[:], psb(4)[:]),
                      reads=[res("ps4")], writes=[res("vts")])
                kv_dst = dram["kvin"][TPC:2 * TPC,
                                      j * 128:(j + 1) * 128].rearrange(
                    "(tt p) d -> p tt d", p=128)
                kb.dma("sync", kv_dst,
                       vts.rearrange("p (tt d) -> p tt d", d=128),
                       reads=[res("vts")], writes=[res("kvin")])
            p1_flush()
            p1_pend[0] = post_v

        p1_flush()
        kb.dma("sync", dram["kvin"][2 * TPC:2 * TPC + 1, 0:8], k2lb,
               reads=[res("k2lb")], writes=[res("kvin")])

        if not SKIP_CC:
            kb.op("gpsimd", lambda g: g.collective_compute(
                "AllGather", ALU.bypass, replica_groups=[list(range(NCORES))],
                ins=[dram["kvin"].opt()], outs=[dram["kvag"].opt()]),
                reads=[res("kvin")], writes=[res("kvag")], cc=True)

        for h in range(NH):
            bi = h % 3
            sl = h % 2
            load_colblock(I["w_q"], h, bi)
            proj_mms(bi)
            evac_slab(sl)

            def post_q(h=h, sl=sl):
                sq_ones(f"osl{sl}", h, "q")
                for half in range(2):
                    kb.op("vector", (lambda half=half: lambda v: v.tensor_copy(
                        q2r[0:1, half * 512:half * 512 + 512],
                        PS[2 + half][0:1, :]))(),
                        reads=[res(f"ps{2 + half}")], writes=[res("q2r")])
                for tt in range(NTT):
                    kb.op("tensor", (lambda tt=tt: lambda t: t.transpose(
                        PS[5][:, tt:tt + 1], q2r[0:1, tt * 128:(tt + 1) * 128],
                        idf[0:1, 0:1]))(),
                        reads=[res("q2r"), res("id_f32")], writes=[res("ps5")])
                kb.op("vector", (lambda h=h: lambda v: v.tensor_copy(
                    q2st[:, h * NTT:(h + 1) * NTT], PS[5][:, 0:NTT]))(),
                    reads=[res("ps5")], writes=[res("q2st")])
                kb.dma("sync", dram["qbf"][h * 128:(h + 1) * 128, :],
                       osl[sl][:],
                       reads=[res(f"osl{sl}")], writes=[res("qbf")])
            p1_flush()
            p1_pend[0] = post_q
        p1_flush()

        if upto < 2:
            _finish_dev(nc, kb, A, out, res, dram, dbg_out)
            kb.finalize()
            return nc, dbg_out

        link(["xfm"], [f"xfm{c}" for c in range(NC)])
        for nm in ("xfm", "wc0", "wc1", "wc2", "osl0", "osl1", "vts", "sqb",
                   "q2r", "k2l", "k2l2", "k2lb"):
            A.release(nm)
        attnfm = A.bf("attnfm", NH * TPC)
        kg = [A.bf(f"kg{i}", S) for i in range(2)]
        vg = [A.bf(f"vg{i}", S) for i in range(2)]
        qh = [A.bf(f"qh{i}", TPC) for i in range(2)]
        pbuf = [A.bf(f"pbuf{i}", 1024) for i in range(12)]
        ptmp = A.bf("ptmp", 1024)
        ptb = [A.bf(f"ptb{i}", 1024) for i in range(2)]
        lparts = A.f32("lparts", 4 * 32)
        lsum4 = A.f32("lsum4", 4)
        linv4 = A.f32("linv4", 4)
        lrow = A.f32("lrow", 512)[0:1, :]
        linvb = A.f32("linvb", 512)
        oraw = [A.f32(f"oraw{i}", 512) for i in range(2)]
        k2g8 = A.bf("k2g8", 64)[0:8, 0:8]
        k2m8 = A.f32("k2m8", 8)[0:8, 0:1]
        k2rowS = A.f32("k2rowS", 8)[0:1, :]
        k2mb = A.f32("k2mb", 8)
        tscr = A.f32("tscr", NH * NTT)
        tscr2 = A.f32("tscr2", NH * NTT)

        link(["pd0"], ["ps0", "ps1"])
        link(["pd1"], ["ps2", "ps3"])
        k2src = bass.AP(dram["kvag"].tensor, 2 * TPC * DKV,
                        [[1, 8], [KVROWS * DKV, 8]])
        kb.dma("sync", k2g8, k2src, reads=[res("kvag")], writes=[res("k2g8")],
               allow_slow_non_contiguous=True)
        kb.op("vector", lambda v: v.reduce_max(k2m8, k2g8, AXX),
              reads=[res("k2g8")], writes=[res("k2m8")])
        kb.op("tensor", lambda t: t.transpose(PS[6][0:1, 0:8], k2m8,
                                              idf[0:8, 0:8]),
              reads=[res("k2m8"), res("id_f32")], writes=[res("ps6")])
        kb.op("vector", lambda v: v.tensor_copy(k2rowS, PS[6][0:1, 0:8]),
              reads=[res("ps6")], writes=[res("k2rowS")])
        for g in range(8):
            kb.op("tensor", (lambda g=g: lambda t: t.matmul(
                PS[6][:, g:g + 1], onerf[:], k2rowS[0:1, g:g + 1],
                start=True, stop=True))(),
                reads=[res("k2rowS"), res("ones_rf")], writes=[res("ps6")])
        kb.op("vector", lambda v: v.tensor_copy(k2mb[:], PS[6][:, 0:8]),
              reads=[res("ps6")], writes=[res("k2mb")])
        for g in range(8):
            sl = slice(g * 32, (g + 1) * 32)
            kb.op("vector", (lambda g=g, sl=sl: lambda v: v.tensor_scalar_mul(
                tscr[:, sl], q2st[:, sl], k2mb[:, g:g + 1]))(),
                reads=[res("q2st"), res("k2mb")], writes=[res("tscr")])
        kb.op("scalar", lambda sc: sc.activation(
            tscr2[:], tscr[:], AF.Ln, scale=SCALE * SCALE),
            reads=[res("tscr")], writes=[res("tscr2")])
        kb.op("scalar", lambda sc: sc.activation(
            tscr[:], tscr2[:], AF.Exp, scale=0.5),
            reads=[res("tscr2")], writes=[res("tscr")])
        kb.op("vector", lambda v: v.tensor_scalar(
            biasall[:], tscr[:], -1.0, -1.0, op0=ALU.mult, op1=ALU.add),
            reads=[res("tscr")], writes=[res("biasall")])

        def load_kv(g, par):
            kgv = kg[par].rearrange("p (j m) -> p j m", j=8)
            for j8 in range(8):
                src = bass.AP(dram["kvag"].tensor,
                              (j8 * KVROWS + g * 128) * DKV,
                              [[DKV, 128], [1, TPC]])
                kb.dma("scalar", kgv[:, j8, :], src,
                       reads=[res("kvag")], writes=[res(f"kg{par}")])
            for j8 in range(8):
                src = bass.AP(dram["kvag"].tensor,
                              ((j8 * KVROWS + TPC) * DKV) + g * 128,
                              [[DKV, 128], [128 * DKV, 8], [1, 128]])
                dstv = vg[par].rearrange("p (l j d) -> p l j d", l=8, j=8)
                kb.dma("scalar", dstv[:, :, j8, :], src,
                       reads=[res("kvag")], writes=[res(f"vg{par}")])

        load_kv(0, 0)
        nheads_bias = res("biasall")

        PTB_B = (4, 5)
        s_ctr = [0]
        pt_ctr = [0]

        for g in range(8):
            par = g % 2
            if g < 7:
                load_kv(g + 1, (g + 1) % 2)
            kgv = kg[par].rearrange("p (j m) -> p j m", j=8)
            for hl in range(4):
                h = 4 * g + hl
                qp = h % 2
                kb.dma("sync", qh[qp][:, :], dram["qbf"][h * 128:(h + 1) * 128, :],
                       reads=[res("qbf")], writes=[res(f"qh{qp}")])
                for m in range(2):
                    tt0 = 4 * m
                    tts = list(range(tt0, tt0 + 4))
                    first_pv = [True]
                    pend = [None]

                    def emit_A_tt(L, tt):
                        sp = s_ctr[0] % 2
                        s_ctr[0] += 1
                        for half in range(2):
                            kb.op("tensor", (lambda sp=sp, tt=tt, L=L,
                                             half=half, qp=qp, kgv=kgv:
                                  lambda t: t.matmul(
                                      PD[sp][:, half * 512:half * 512 + 512],
                                      qh[qp][:, tt * 128:(tt + 1) * 128],
                                      kgv[:, half * 4:half * 4 + 4,
                                          L * 128:(L + 1) * 128],
                                      start=True, stop=True))(),
                                reads=[res(f"qh{qp}"), res(f"kg{par}")],
                                writes=[res(f"pd{sp}")])
                        dl = (tt - tt0) * 3 + (L % 3)
                        bi = L
                        bcol = biasall[:, h * 8 + tt:h * 8 + tt + 1]
                        if L < tt:
                            kb.op("scalar", (lambda sp=sp, dl=dl, bcol=bcol,
                                             tt=tt, bi=bi:
                                  lambda sc: sc.activation(
                                      pbuf[dl][:],
                                      PD[sp][:], AF.Exp, bias=bcol,
                                      scale=SCALE,
                                      accum_out=lparts[:, (tt - tt0) * 32
                                                       + bi:
                                                       (tt - tt0) * 32
                                                       + bi + 1]))(),
                                reads=[res(f"pd{sp}"), nheads_bias],
                                writes=[res(f"pbuf{dl}"), res("lparts")])
                        else:
                            kb.op("scalar", (lambda sp=sp, bcol=bcol:
                                  lambda sc: sc.activation(
                                      ptmp[:], PD[sp][:], AF.Exp,
                                      bias=bcol, scale=SCALE))(),
                                reads=[res(f"pd{sp}"), nheads_bias],
                                writes=[res("ptmp")])
                            kb.op("vector", (lambda dl=dl, tt=tt, bi=bi:
                                  lambda v: v.scalar_tensor_tensor(
                                      pbuf[dl][:],
                                      ptmp[:], 1.0, maskb[:],
                                      op0=ALU.mult, op1=ALU.mult,
                                      accum_out=lparts[:,
                                          (tt - tt0) * 32 + bi:
                                          (tt - tt0) * 32 + bi + 1]))(),
                                reads=[res("ptmp"), res("dmask")],
                                writes=[res(f"pbuf{dl}"), res("lparts")])

                    def flush_pend():
                        if not pend[0]:
                            return
                        (ptp, tbp, pvs) = pend[0]
                        pend[0] = None
                        kb.op("vector", (lambda ptp=ptp, tbp=tbp: lambda v:
                              v.tensor_copy(ptb[tbp][:], psb(ptp)[:]))(),
                            reads=[res(f"ps{ptp}")],
                            writes=[res(f"ptb{tbp}")])
                        for (chi, jh, col0, ncols, st, pv_par) in pvs:
                            kb.op("tensor", (lambda chi=chi, tbp=tbp, jh=jh,
                                             col0=col0, ncols=ncols, st=st,
                                             pv_par=pv_par: lambda t: t.matmul(
                                PS[6][:, col0:col0 + ncols],
                                vg[pv_par][:, chi * 128:(chi + 1) * 128],
                                ptb[tbp][:, jh * 512:jh * 512 + ncols],
                                start=st, stop=False,
                                skip_group_check=True))(),
                                reads=[res(f"ptb{tbp}"), res(f"vg{pv_par}")],
                                writes=[res("ps6")])

                    def emit_B_j8p(L, j8p):
                        vtts = [tt for tt in tts if tt >= L]
                        if not vtts:
                            return
                        ncols = len(vtts) * 128
                        col0 = (vtts[0] - tt0) * 128
                        ptp = PTB_B[pt_ctr[0] % 2]
                        tbp = pt_ctr[0] % 2
                        pt_ctr[0] += 1
                        for jh in range(2):
                            j8 = 2 * j8p + jh
                            for ix, tt in enumerate(vtts):
                                dl = (tt - tt0) * 3 + (L % 3)
                                kb.op("tensor", (lambda ptp=ptp, ix=ix,
                                                 dl=dl, j8=j8, jh=jh:
                                      lambda t: t.transpose(
                                    psb(ptp)[:, jh * 512 + ix * 128:
                                             jh * 512 + (ix + 1) * 128],
                                    pbuf[dl][:, j8 * 128:(j8 + 1) * 128],
                                    idb[:]))(),
                                    reads=[res(f"pbuf{dl}"), res("id_bf")],
                                    writes=[res(f"ps{ptp}")])
                        pvs = []
                        for jh in range(2):
                            j8 = 2 * j8p + jh
                            chi = L * 8 + j8
                            st = first_pv[0]
                            first_pv[0] = False
                            pvs.append((chi, jh, col0, ncols, st, par))
                        prev = pend[0]
                        pend[0] = (ptp, tbp, pvs)
                        if prev is not None:
                            tmp = pend[0]
                            pend[0] = prev
                            flush_pend()
                            pend[0] = tmp

                    Lmax = tt0 + 3
                    for tt in tts:
                        emit_A_tt(0, tt)
                    for tt in tts:
                        if tt >= 1:
                            emit_A_tt(1, tt)
                    for L in range(2, Lmax + 1):
                        amini = [tt for tt in tts if tt >= L]
                        for i in range(4):
                            if i < len(amini):
                                emit_A_tt(L, amini[i])
                            emit_B_j8p(L - 2, i)
                    emit_B_j8p(Lmax - 1, 0)
                    emit_B_j8p(Lmax - 1, 1)
                    emit_B_j8p(Lmax - 1, 2)
                    emit_B_j8p(Lmax - 1, 3)
                    for i in range(4):
                        emit_B_j8p(Lmax, i)
                    flush_pend()
                    orp = (h * 2 + m) % 2
                    kb.op("vector", (lambda orp=orp: lambda v: v.tensor_copy(
                        oraw[orp][:], PS[6][:]))(),
                        reads=[res("ps6")], writes=[res(f"oraw{orp}")])
                    for ix, tt in enumerate(tts):
                        kb.op("vector", (lambda ix=ix, nbi=tt + 1:
                              lambda v: v.reduce_sum(
                                  lsum4[:, ix:ix + 1],
                                  lparts[:, ix * 32:ix * 32 + nbi],
                                  AXX))(),
                            reads=[res("lparts")], writes=[res("lsum4")])
                    kb.op("vector", lambda v: v.reciprocal(
                        linv4[:], lsum4[:]),
                        reads=[res("lsum4")], writes=[res("linv4")])
                    for ix in range(4):
                        kb.op("tensor", (lambda ix=ix: lambda t: t.transpose(
                            PS[7][0:1, ix * 128:(ix + 1) * 128],
                            linv4[:, ix:ix + 1], idf[:]))(),
                            reads=[res("linv4"), res("id_f32")],
                            writes=[res("ps7")])
                    kb.op("vector", lambda v: v.tensor_copy(lrow, PS[7][0:1, :]),
                          reads=[res("ps7")], writes=[res("lrow")])
                    kb.op("tensor", lambda t: t.matmul(
                        PS[7][:], onerf[:], lrow, start=True, stop=True),
                        reads=[res("lrow"), res("ones_rf")], writes=[res("ps7")])
                    kb.op("vector", lambda v: v.tensor_copy(linvb[:], PS[7][:]),
                          reads=[res("ps7")], writes=[res("linvb")])
                    acol = h * TPC + m * 512
                    kb.op("vector", (lambda acol=acol, orp=orp: lambda v:
                          v.tensor_tensor(attnfm[:, acol:acol + 512],
                                          oraw[orp][:], linvb[:],
                                          op=ALU.mult))(),
                        reads=[res(f"oraw{orp}"), res("linvb")],
                        writes=[res("attnfm")])
        if dbg:
            for hc in range(NH):
                kb.dma("sync", dbg_out["attn"][:, hc * TPC:(hc + 1) * TPC],
                       attnfm[:, hc * TPC:(hc + 1) * TPC],
                       reads=[res("attnfm")], writes=[res("outw")])
            kb.dma("sync", dbg_out["misc"][:, 0:256], biasall[:],
                   reads=[res("biasall")], writes=[res("outw")])
            kb.dma("sync", dbg_out["misc"][:, 256:512], q2st[:],
                   reads=[res("q2st")], writes=[res("outw")])
            kb.dma("sync", dbg_out["misc"][:, 512:520], k2mb[:],
                   reads=[res("k2mb")], writes=[res("outw")])
            kb.dma("sync", dbg_out["misc"][0:8, 520:524],
                   k2g8.bitcast(F32)[0:8, 0:4],
                   reads=[res("k2g8")], writes=[res("outw")])

        if upto < 3:
            _finish_dev(nc, kb, A, out, res, dram, dbg_out)
            kb.finalize()
            return nc, dbg_out

        for nm in ("kg0", "kg1", "vg0", "vg1", "qh0", "qh1", "ptmp",
                   "ptb0", "ptb1", "lparts", "lsum4", "linv4", "lrow", "linvb",
                   "oraw0", "oraw1",
                   "k2g8", "k2m8", "k2rowS", "k2mb", "tscr", "tscr2"):
            A.release(nm)
        for i in range(12):
            A.release(f"pbuf{i}")

        link(["ps0", "ps1"], ["pd0"])
        link(["ps2", "ps3"], ["pd1"])
        woc = [A.bf(f"woc{i}", D) for i in range(2)]
        tmpo = [A.f32(f"tmpo{i}", TPC) for i in range(2)]
        tmpz = [A.f32(f"tmpz{i}", TPC) for i in range(2)]

        def fm_to_tm_phase(rhs_sb, rhs_res, lhs_loader, dst_dram, dst_res,
                           extra_mms=None):
            def emit_tail(oc):
                zi = oc % 2
                for tt in range(NTT):
                    pb_ = 2 + tt % 2
                    kb.op("tensor", (lambda pb_=pb_, tt=tt, zi=zi: lambda t:
                          t.transpose(PS[pb_][:, 0:128],
                                      tmpo[zi][:, tt * 128:(tt + 1) * 128],
                                      idf[:]))(),
                        reads=[res(f"tmpo{zi}"), res("id_f32")],
                        writes=[res(f"ps{pb_}")])
                    kb.op("scalar", (lambda pb_=pb_, tt=tt, zi=zi: lambda sc:
                          sc.copy(tmpz[zi][:, tt * 128:(tt + 1) * 128],
                                  PS[pb_][:, 0:128]))(),
                        reads=[res(f"ps{pb_}")], writes=[res(f"tmpz{zi}")])
                dstv = dst_dram.rearrange("(tt p) d -> p tt d", p=128)
                for th in range(2):
                    kb.dma("sync", dstv[:, th * 4:(th + 1) * 4,
                                        oc * 128:(oc + 1) * 128],
                           tmpz[zi][:, th * 512:(th + 1) * 512].rearrange(
                               "p (tt o) -> p tt o", o=128),
                           reads=[res(f"tmpz{zi}")],
                           writes=[res(dst_res + str(th))])

            for oc in range(NC):
                bi = oc % 2
                lhs_loader(oc, bi)
                for half in range(2):
                    for c in range(NC):
                        kb.op("tensor", (lambda bi=bi, c=c, half=half:
                              lambda t: t.matmul(
                                  PS[half][:],
                                  woc[bi][:, c * 128:(c + 1) * 128],
                                  rhs_sb[:, c * TPC + half * 512:
                                         c * TPC + half * 512 + 512],
                                  start=(c == 0),
                                  stop=(c == NC - 1 and extra_mms is None)))(),
                            reads=[res(f"woc{bi}"), rhs_res],
                            writes=[res(f"ps{half}")])
                    if extra_mms is not None:
                        extra_mms(oc, half)
                    zi0 = oc % 2
                    kb.op("vector", (lambda half=half, zi=zi0: lambda v:
                          v.tensor_copy(tmpo[zi][:, half * 512:half * 512 + 512],
                                        PS[half][:]))(),
                        reads=[res(f"ps{half}")], writes=[res(f"tmpo{zi0}")])
                if oc > 0:
                    emit_tail(oc - 1)
            emit_tail(NC - 1)

        def wo_loader(oc, bi):
            wv = I["w_o"][:, oc * 128:(oc + 1) * 128].rearrange(
                "(c p) o -> p c o", p=128)
            bv = woc[bi].rearrange("p (c o) -> p c o", o=128)
            for gix in range(4):
                kb.dma("gpsimd", bv[:, gix * 8:(gix + 1) * 8, :],
                       wv[:, gix * 8:(gix + 1) * 8, :], writes=[res(f"woc{bi}")])

        fm_to_tm_phase(attnfm, res("attnfm"), wo_loader, dram["ztm"], "ztm")

        if upto < 4:
            _finish_dev(nc, kb, A, out, res, dram, dbg_out)
            kb.finalize()
            return nc, dbg_out

        A.release("attnfm")
        A.release("woc0")
        A.release("woc1")
        A.release("tmpo0")
        A.release("tmpo1")
        A.release("tmpz0")
        A.release("tmpz1")

        def ln_phase(src_dram, src_res_pfx, res_dram, store_fn):
            bufa = [A.f32(f"lnA{i}", D) for i in range(3)]
            bufb = [A.f32(f"lnB{i}", D) for i in range(3)]
            bufz = [A.f32(f"lnZ{i}", D) for i in range(3)]
            svecs = [A.f32(f"lnS{i}", 16) for i in range(3)]
            for tt in range(NTT):
                p_ = tt % 3
                svec = svecs[p_]
                rz = res(f"lnZ{p_}")
                ra = res(f"lnA{p_}")
                rb = res(f"lnB{p_}")
                rs = res(f"lnS{p_}")
                kb.dma("sync", bufa[p_][:], src_dram[tt * 128:(tt + 1) * 128, :],
                       reads=[res(src_res_pfx + str(tt // 4))], writes=[ra])
                kb.dma("sync", bufb[p_][:],
                       res_dram[tt * 128:(tt + 1) * 128, :],
                       reads=[res("y1f")] if res_dram is dram["y1f"] else [],
                       writes=[rb])
                kb.op("vector", (lambda p_=p_: lambda v: v.tensor_add(
                    bufz[p_][:], bufa[p_][:], bufb[p_][:]))(),
                    reads=[ra, rb], writes=[rz])
                kb.op("vector", (lambda p_=p_: lambda v: v.reduce_sum(
                    svec[:, 0:1], bufz[p_][:], AXX))(),
                    reads=[rz], writes=[rs])
                kb.op("scalar", (lambda p_=p_: lambda sc: sc.activation(
                    bufa[p_][:], bufz[p_][:], AF.Square))(),
                    reads=[rz], writes=[ra])
                kb.op("vector", (lambda p_=p_: lambda v: v.reduce_sum(
                    svec[:, 1:2], bufa[p_][:], AXX))(),
                    reads=[ra], writes=[rs])
                kb.op("vector", lambda v: v.tensor_scalar_mul(
                    svec[:, 2:3], svec[:, 0:1], 1.0 / D),
                    reads=[rs], writes=[rs])
                kb.op("scalar", lambda sc: sc.activation(
                    svec[:, 3:4], svec[:, 2:3], AF.Square),
                    reads=[rs], writes=[rs])
                kb.op("vector", lambda v: v.tensor_scalar(
                    svec[:, 4:5], svec[:, 1:2], 1.0 / D, None, op0=ALU.mult),
                    reads=[rs], writes=[rs])
                kb.op("vector", lambda v: v.tensor_sub(
                    svec[:, 5:6], svec[:, 4:5], svec[:, 3:4]),
                    reads=[rs], writes=[rs])
                kb.op("scalar", lambda sc: sc.activation(
                    svec[:, 6:7], svec[:, 5:6], AF.Ln, bias=epsv[:]),
                    reads=[rs, res("epsv")], writes=[rs])
                kb.op("scalar", lambda sc: sc.activation(
                    svec[:, 7:8], svec[:, 6:7], AF.Exp, scale=-0.5),
                    reads=[rs], writes=[rs])
                kb.op("vector", lambda v: v.tensor_mul(
                    svec[:, 8:9], svec[:, 2:3], svec[:, 7:8]),
                    reads=[rs], writes=[rs])
                kb.op("vector", (lambda p_=p_: lambda v: v.tensor_scalar(
                    bufb[p_][:], bufz[p_][:], svec[:, 2:3], svec[:, 7:8],
                    op0=ALU.subtract, op1=ALU.mult))(),
                    reads=[rz, rs], writes=[rb])
                store_fn(tt, bufb[p_], rb)
            for i in range(3):
                A.release(f"lnA{i}")
                A.release(f"lnB{i}")
                A.release(f"lnZ{i}")
                A.release(f"lnS{i}")

        def y1_store(tt, buf, rbuf):
            kb.dma("scalar", dram["y1f"][tt * 128:(tt + 1) * 128, :], buf[:],
                   reads=[rbuf], writes=[res("y1f")])
            kb.dma("gpsimd", dram["y1bf"][tt * 128:(tt + 1) * 128, :], buf[:],
                   reads=[rbuf], writes=[res(f"y1bf{tt // 4}")])

        ln_phase(dram["ztm"], "ztm", I["x_own"], y1_store)

        if upto < 5:
            _finish_dev(nc, kb, A, out, res, dram, dbg_out)
            kb.finalize()
            return nc, dbg_out

        yfm = A.bf("yfm", NC * TPC)
        wbt = [A.bf(f"wbt{i}", NC * 512) for i in range(2)]
        bstk = A.bf("bstk", D)
        logf = A.f32("logf", TPC)[0:8, :]
        ltm = A.f32("ltm", 8)
        e8t = A.f32("e8t", 8)
        cnt = A.f32("cnt", 8)
        tj8 = A.f32("tj8", 8)
        wmsk = A.f32("wmsk", 8)
        wt8 = A.f32("wt8", 8)
        sc8 = A.f32("sc8", 8)
        wfmb = A.bf("wfmb", TPC)[0:8, :]
        wbs = A.f32("wbs", TPC)
        hs = A.f32("hs", TPC)
        gsb = A.bf("gsb", TPC)
        tmpo5 = [A.f32(f"tmpo{i}", TPC) for i in range(2)]
        tmpz5 = [A.f32(f"tmpz{i}", TPC) for i in range(2)]

        kb.dma("sync", bstk[:], dram["bstk_d"],
               reads=[res("bstk_d")], writes=[res("bstk")])
        for c in range(NC):
            for th in range(2):
                kb.dma("sync",
                       yfm[:, c * TPC + th * 512: c * TPC + th * 512 + 512],
                       dram["y1bf"][th * 512:(th + 1) * 512,
                                    c * 128:(c + 1) * 128],
                       reads=[res(f"y1bf{th}")], writes=[res("yfm")],
                       transpose=True)

        for half in range(2):
            for c in range(NC):
                kb.op("tensor", (lambda c=c, half=half: lambda t: t.matmul(
                    PS[0][0:8, :], wrtT[:, c * 8:(c + 1) * 8],
                    yfm[:, c * TPC + half * 512: c * TPC + half * 512 + 512],
                    start=(c == 0), stop=(c == NC - 1)))(),
                    reads=[res("wrtT"), res("yfm")], writes=[res("ps0")])
            kb.op("vector", (lambda half=half: lambda v: v.tensor_copy(
                logf[:, half * 512:half * 512 + 512], PS[0][0:8, :]))(),
                reads=[res("ps0")], writes=[res("logf")])

        for tt in range(NTT):
            kb.op("tensor", (lambda tt=tt: lambda t: t.matmul(
                PS[1][:, 0:8], logf[0:8, tt * 128:(tt + 1) * 128],
                idf[0:8, 0:8], start=True, stop=True))(),
                reads=[res("logf"), res("id_f32")], writes=[res("ps1")])
            kb.op("vector", lambda v: v.tensor_copy(ltm[:], PS[1][:, 0:8]),
                  reads=[res("ps1")], writes=[res("ltm")])
            kb.op("vector", lambda v: v.reduce_max(sc8[:, 0:1], ltm[:], AXX),
                  reads=[res("ltm")], writes=[res("sc8")])
            kb.op("vector", lambda v: v.tensor_scalar_mul(
                sc8[:, 1:2], sc8[:, 0:1], -1.0),
                reads=[res("sc8")], writes=[res("sc8")])
            kb.op("scalar", lambda sc: sc.activation(
                e8t[:], ltm[:], AF.Exp, bias=sc8[:, 1:2]),
                reads=[res("ltm"), res("sc8")], writes=[res("e8t")])
            kb.op("vector", lambda v: v.memset(cnt[:], 0.0),
                  writes=[res("cnt")])
            for j in range(E):
                kb.op("vector", (lambda j=j: lambda v: v.tensor_single_scalar(
                    tj8[:], e8t[:], e8t[:, j:j + 1], op=ALU.is_lt))(),
                    reads=[res("e8t")], writes=[res("tj8")])
                kb.op("vector", lambda v: v.tensor_add(cnt[:], cnt[:], tj8[:]),
                      reads=[res("tj8"), res("cnt")], writes=[res("cnt")])
            kb.op("vector", lambda v: v.tensor_single_scalar(
                wmsk[:], cnt[:], 2.0, op=ALU.is_lt),
                reads=[res("cnt")], writes=[res("wmsk")])
            kb.op("vector", lambda v: v.tensor_mul(wt8[:], e8t[:], wmsk[:]),
                  reads=[res("e8t"), res("wmsk")], writes=[res("wt8")])
            kb.op("vector", lambda v: v.reduce_sum(sc8[:, 2:3], wt8[:], AXX),
                  reads=[res("wt8")], writes=[res("sc8")])
            kb.op("vector", lambda v: v.reciprocal(sc8[:, 3:4], sc8[:, 2:3]),
                  reads=[res("sc8")], writes=[res("sc8")])
            kb.op("vector", lambda v: v.tensor_scalar_mul(
                wt8[:], wt8[:], sc8[:, 3:4]),
                reads=[res("wt8"), res("sc8")], writes=[res("wt8")])
            kb.op("tensor", lambda t: t.transpose(PS[1][0:8, 0:128], wt8[:],
                                                  idf[:]),
                  reads=[res("wt8"), res("id_f32")], writes=[res("ps1")])
            kb.op("vector", (lambda tt=tt: lambda v: v.tensor_copy(
                wfmb[:, tt * 128:(tt + 1) * 128], PS[1][0:8, 0:128]))(),
                reads=[res("ps1")], writes=[res("wfmb")])

        for half in range(2):
            kb.op("tensor", (lambda half=half: lambda t: t.matmul(
                PS[2][:], e8b[:], wfmb[:, half * 512:half * 512 + 512],
                start=True, stop=True))(),
                reads=[res("wfmb"), res("e8")], writes=[res("ps2")])
            kb.op("vector", (lambda half=half: lambda v: v.tensor_copy(
                wbs[:, half * 512:half * 512 + 512], PS[2][:]))(),
                reads=[res("ps2")], writes=[res("wbs")])
            for c in range(NC):
                kb.op("tensor", (lambda c=c, half=half: lambda t: t.matmul(
                    PS[3][:], astk[:, c * 128:(c + 1) * 128],
                    yfm[:, c * TPC + half * 512: c * TPC + half * 512 + 512],
                    start=(c == 0), stop=(c == NC - 1)))(),
                    reads=[res("astk"), res("yfm")], writes=[res("ps3")])
            kb.op("vector", (lambda half=half: lambda v: v.tensor_copy(
                hs[:, half * 512:half * 512 + 512], PS[3][:]))(),
                reads=[res("ps3")], writes=[res("hs")])
        kb.op("vector", lambda v: v.tensor_mul(gsb[:], hs[:], wbs[:]),
              reads=[res("hs"), res("wbs")], writes=[res("gsb")])

        def load_wbt(ob):
            par = ob % 2
            for c in range(NC):
                kb.dma("sync", wbt[par][:, c * 512:(c + 1) * 512],
                       dram["wbase_bf"][ob * 512:(ob + 1) * 512,
                                        c * 128:(c + 1) * 128],
                       reads=[res(f"wbase_bf{ob}")],
                       writes=[res(f"wbt{par}")], transpose=True)

        load_wbt(0)
        load_wbt(1)

        def wb_loader(oc, bi):
            pass

        def wb_lhs(oc):
            ob = oc // 4
            par = ob % 2
            oc4 = oc % 4
            return wbt[par], par, oc4

        def p5_loader(oc, bi):
            wb_loader(oc, bi)

        def p5_extra(oc, half):
            kb.op("tensor", (lambda oc=oc, half=half: lambda t: t.matmul(
                PS[half][:], bstk[:, oc * 128:(oc + 1) * 128],
                gsb[:, half * 512:half * 512 + 512],
                start=False, stop=False, skip_group_check=True))(),
                reads=[res("bstk"), res("gsb")], writes=[res(f"ps{half}")])
            kb.op("tensor", (lambda oc=oc, half=half: lambda t: t.matmul(
                PS[half][:], brow[0:1, oc * 128:(oc + 1) * 128],
                oner[:], start=False, stop=True, skip_group_check=True))(),
                reads=[res("brow"), res("ones_row")], writes=[res(f"ps{half}")])

        def p5_tail(oc):
            zi = oc % 2
            for tt in range(NTT):
                pb_ = 2 + tt % 2
                kb.op("tensor", (lambda pb_=pb_, tt=tt, zi=zi: lambda t:
                      t.transpose(PS[pb_][:, 0:128],
                                  tmpo5[zi][:, tt * 128:(tt + 1) * 128],
                                  idf[:]))(),
                    reads=[res(f"tmpo{zi}"), res("id_f32")],
                    writes=[res(f"ps{pb_}")])
                kb.op("scalar", (lambda pb_=pb_, tt=tt, zi=zi: lambda sc:
                      sc.copy(tmpz5[zi][:, tt * 128:(tt + 1) * 128],
                              PS[pb_][:, 0:128]))(),
                    reads=[res(f"ps{pb_}")], writes=[res(f"tmpz{zi}")])
            dstv = dram["fftm"].rearrange("(tt p) d -> p tt d", p=128)
            for th in range(2):
                kb.dma("sync", dstv[:, th * 4:(th + 1) * 4,
                                    oc * 128:(oc + 1) * 128],
                       tmpz5[zi][:, th * 512:(th + 1) * 512].rearrange(
                           "p (tt o) -> p tt o", o=128),
                       reads=[res(f"tmpz{zi}")], writes=[res(f"fftm{th}")])

        for oc in range(NC):
            p5_loader(oc, oc % 2)
            slab, par, oc4 = wb_lhs(oc)
            for half in range(2):
                for c in range(NC):
                    kb.op("tensor", (lambda slab=slab, par=par, oc4=oc4, c=c,
                                     half=half: lambda t: t.matmul(
                        PS[half][:],
                        slab[:, c * 512 + oc4 * 128: c * 512 + oc4 * 128 + 128],
                        yfm[:, c * TPC + half * 512: c * TPC + half * 512 + 512],
                        start=(c == 0), stop=False,
                        skip_group_check=True))(),
                        reads=[res(f"wbt{par}"), res("yfm")],
                        writes=[res(f"ps{half}")])
                p5_extra(oc, half)
                zi0 = oc % 2
                kb.op("vector", (lambda half=half, zi=zi0: lambda v:
                      v.tensor_copy(tmpo5[zi][:, half * 512:half * 512 + 512],
                                    PS[half][:]))(),
                    reads=[res(f"ps{half}")], writes=[res(f"tmpo{zi0}")])
            if oc > 0:
                p5_tail(oc - 1)
            if oc % 4 == 3 and oc // 4 + 2 < 8:
                load_wbt(oc // 4 + 2)
        p5_tail(NC - 1)

        if upto < 6:
            _finish_dev(nc, kb, A, out, res, dram, dbg_out)
            kb.finalize()
            return nc, dbg_out

        for nm in ("yfm", "wbt0", "wbt1", "bstk", "logf", "ltm", "e8t", "cnt",
                   "tj8", "wmsk", "wt8", "sc8", "wfmb", "wbs", "hs", "gsb",
                   "tmpo0", "tmpo1", "tmpz0", "tmpz1"):
            A.release(nm)

        def out_store(tt, buf, rbuf):
            kb.dma("scalar", out[tt * 128:(tt + 1) * 128, :], buf[:],
                   reads=[rbuf], writes=[res("outw")])

        ln_phase(dram["fftm"], "fftm", dram["y1f"], out_store)
        kb.op("sync", lambda e: e.nop(), reads=[res("outw")])

        kb.finalize()
    return nc, dbg_out


def _finish_dev(nc, kb, A, out, res, dram=None, dbg_map=None):
    zz = A.f32("zz_dev", 1024)
    kb.op("gpsimd", lambda g: g.memset(zz[:], 0.0), writes=[res("zz_dev")])
    for tt in range(NTT):
        for q4 in range(4):
            kb.dma("gpsimd", out[tt * 128:(tt + 1) * 128,
                                 q4 * 1024:(q4 + 1) * 1024], zz[:],
                   reads=[res("zz_dev")], writes=[res("outw")])
    if dbg_map:
        for nm, dst in dbg_map.items():
            if nm not in dram:
                continue
            kb.dma("sync", dst, dram[nm], reads=[res(nm)], writes=[res("outw")])
    kb.op("gpsimd", lambda g: g.engine_nop(), reads=[res("outw")])


def _host_consts(core_i):
    m = np.zeros((128, 1024), dtype=np.float32)
    for j8 in range(8):
        k = 0 if j8 <= core_i else -1
        m[:, j8 * 128:(j8 + 1) * 128] = np.tril(np.ones((128, 128), np.float32), k)
    e8v = np.zeros((E, 128), dtype=np.float32)
    for e_ix in range(E):
        e8v[e_ix, e_ix * RLO:(e_ix + 1) * RLO] = MOE_SCALING
    return {
        "id_bf": np.eye(128, dtype=ml_dtypes.bfloat16),
        "id_f32": np.eye(128, dtype=np.float32),
        "dmask": m.astype(ml_dtypes.bfloat16),
        "ones_col": np.ones((128, 1), dtype=ml_dtypes.bfloat16),
        "ones_row": np.ones((1, 512), dtype=ml_dtypes.bfloat16),
        "ones_rf": np.ones((1, 128), dtype=np.float32),
        "e8": e8v.astype(ml_dtypes.bfloat16),
    }


def make_in_maps(x, wq, wk, wv, wo, wbase, bbase, wrt, la, lbv):
    shared = {
        "w_q": np.ascontiguousarray(wq, dtype=np.float32),
        "w_k": np.ascontiguousarray(wk, dtype=np.float32),
        "w_v": np.ascontiguousarray(wv, dtype=np.float32),
        "w_o": np.ascontiguousarray(wo, dtype=np.float32),
        "w_base": np.ascontiguousarray(wbase, dtype=np.float32),
        "b_base": np.ascontiguousarray(bbase, dtype=np.float32).reshape(1, D),
        "w_rt": np.ascontiguousarray(wrt, dtype=np.float32),
        "lora_a": np.ascontiguousarray(la, dtype=np.float32).reshape(E * RLO, D),
        "lora_b": np.ascontiguousarray(lbv, dtype=np.float32),
    }
    in_maps = []
    for i in range(NCORES):
        mm = dict(shared)
        mm["x_own"] = np.ascontiguousarray(x[i::8, :], dtype=np.float32)
        mm.update(_host_consts(i))
        in_maps.append(mm)
    return in_maps


def kernel(L0_attn_hidden_states, L0_attn_w_q, L0_attn_w_k, L0_attn_w_v,
           L0_attn_w_o, L0_ffn_W_base, L0_ffn_b_base, L0_ffn_W_router,
           L0_ffn_lora_A, L0_ffn_lora_B):
    x = np.asarray(L0_attn_hidden_states).reshape(S, D)
    nc, _ = build_nc()
    in_maps = make_in_maps(x, L0_attn_w_q, L0_attn_w_k, L0_attn_w_v,
                           L0_attn_w_o, L0_ffn_W_base, L0_ffn_b_base,
                           L0_ffn_W_router, L0_ffn_lora_A, L0_ffn_lora_B)
    res_ = run_bass_kernel_spmd(nc, in_maps, core_ids=list(range(NCORES)))
    full = np.empty((S, D), dtype=np.float32)
    for i in range(NCORES):
        full[i::8, :] = res_.results[i]["out"]
    return full.reshape(1, S, D)


# version 20 (final)
# speedup vs baseline: 1.1592x; 1.0427x over previous
import math
import numpy as np
import ml_dtypes
import concourse.bass as bass
import concourse.mybir as mybir
from concourse.bass_utils import run_bass_kernel_spmd

F32 = mybir.dt.float32
BF16 = mybir.dt.bfloat16
AF = mybir.ActivationFunctionType
ALU = mybir.AluOpType
AXX = mybir.AxisListType.X

NCORES = 8
S = 8192
D = 4096
NH = 32
NKV = 8
HD = 128
DKV = NKV * HD
E = 8
RLO = 16
TPC = S // NCORES
NTT = TPC // 128
NC = D // 128
SCALE = 1.0 / math.sqrt(HD)
EPS = 1e-5
MOE_SCALING = 2.0
KVROWS = 2 * TPC + 1
SKIP_CC = False


class Res:

    __slots__ = ("name", "writers", "readers")

    def __init__(self, name=""):
        self.name = name
        self.writers = {}
        self.readers = {}


class Op:
    __slots__ = ("idx", "eng", "fn", "dma", "deps", "needed", "semval",
                 "semkey", "kw")

    def __init__(self, idx, eng, fn, dma):
        self.idx = idx
        self.eng = eng
        self.fn = fn
        self.dma = dma
        self.deps = set()
        self.needed = False
        self.semval = None


class KB:

    ENGINES = ("sync", "scalar", "vector", "tensor", "gpsimd")

    def __init__(self, nc, ctx):
        self.nc = nc
        self.ctx = ctx
        self.ops = []
        self.phase_marks = [0]
        self.NSLOT = 8
        self.slot_ctr = {}
        self.sems = {}
        for e in self.ENGINES:
            self.sems[(e, "c")] = ctx.enter_context(nc.semaphore(f"sc_{e}"))
            for r in range(self.NSLOT):
                self.sems[(e, "q", r)] = ctx.enter_context(
                    nc.semaphore(f"sq_{e}{r}"))
        self.sems[("gpsimd", "cc")] = ctx.enter_context(nc.semaphore("s_cc"))

    def op(self, eng, fn, reads=(), writes=(), dma=False, cc=False):
        kind = "cc" if cc else ("q" if dma else "c")
        o = Op(len(self.ops), eng, fn, kind)
        if kind == "q":
            r_ = self.slot_ctr.get(eng, 0)
            self.slot_ctr[eng] = r_ + 1
            o.semkey = (eng, "q", r_ % self.NSLOT)
        else:
            o.semkey = (eng, kind)
        deps = set()
        for r in reads:
            deps.update(r.writers.values())
        for wres in writes:
            deps.update(wres.writers.values())
            deps.update(wres.readers.values())
        for d in list(deps):
            dop = self.ops[d]
            if (dop.eng == eng == "tensor" and dop.dma == "c"
                    and kind == "c"):
                deps.discard(d)
        o.deps = deps
        self.ops.append(o)
        for r in reads:
            prev = r.readers.get(o.semkey, -1)
            if o.idx > prev:
                r.readers[o.semkey] = o.idx
        for wres in writes:
            prev = wres.writers.get(o.semkey, -1)
            if o.idx > prev:
                wres.writers[o.semkey] = o.idx
            wres.readers = {}
        return o

    def dma(self, q, out, in_, reads=(), writes=(), **kw):
        return self.op(q, lambda e: e.dma_start(out=out, in_=in_, **kw),
                       reads, writes, dma=True)

    def phase(self):
        self.phase_marks.append(len(self.ops))

    def finalize(self):
        for o in self.ops:
            if o.dma != "c":
                o.needed = True
            for d in o.deps:
                self.ops[d].needed = True
        counts = {}
        for o in self.ops:
            if o.needed:
                counts[o.semkey] = counts.get(o.semkey, 0) + (
                    16 if o.dma == "q" else 1)
                o.semval = counts[o.semkey]
        marks = self.phase_marks + [len(self.ops)]
        for pi in range(len(marks) - 1):
            lo, hi = marks[pi], marks[pi + 1]
            if lo == hi:
                continue
            chunk = self.ops[lo:hi]
            with self.nc.Block() as blk:
                for eng_name in self.ENGINES:
                    my = [o for o in chunk if o.eng == eng_name]
                    if not my:
                        continue
                    self._emit_engine(blk, eng_name, my)

    def _emit_engine(self, blk, eng_name, my):
        ops_all = self.ops
        sems = self.sems
        waited = {}

        def body(eng):
            for o in my:
                if o.needed and o.dma == "q":
                    key = o.semkey
                    v = o.semval - 16
                    if v > 0 and waited.get(key, 0) < v:
                        eng.wait_ge(sems[key], v)
                        waited[key] = v
                for d in sorted(o.deps):
                    dop = ops_all[d]
                    key = dop.semkey
                    v = dop.semval
                    if waited.get(key, 0) < v:
                        eng.wait_ge(sems[key], v)
                        waited[key] = v
                instr = o.fn(eng)
                if o.needed:
                    instr.then_inc(sems[o.semkey],
                                   16 if o.dma == "q" else 1)
        getattr(blk, eng_name)(body)


class Arena:

    def __init__(self, tensor, nbytes, resfn):
        self.t = tensor
        self.free = [(0, nbytes)]
        self.allocs = {}
        self.resfn = resfn
        self.retired = []

    def alloc(self, name, nbytes, align=64):
        for i, (off, ln) in enumerate(self.free):
            aoff = (off + align - 1) // align * align
            if aoff + nbytes <= off + ln:
                pre = (off, aoff - off)
                post = (aoff + nbytes, off + ln - (aoff + nbytes))
                repl = [seg for seg in (pre, post) if seg[1] > 0]
                self.free[i:i + 1] = repl
                self.allocs[name] = (aoff, nbytes)
                r_new = self.resfn(name)
                keep = []
                for (o2, l2, r2) in self.retired:
                    if o2 < aoff + nbytes and aoff < o2 + l2:
                        for k, v in r2.readers.items():
                            if r_new.readers.get(k, -1) < v:
                                r_new.readers[k] = v
                        for k, v in r2.writers.items():
                            if r_new.writers.get(k, -1) < v:
                                r_new.writers[k] = v
                        if not (o2 >= aoff and o2 + l2 <= aoff + nbytes):
                            keep.append((o2, l2, r2))
                    else:
                        keep.append((o2, l2, r2))
                self.retired = keep
                return aoff
        raise MemoryError(f"sbuf arena: cannot alloc {name} ({nbytes} B); "
                          f"free={self.free}")

    def release(self, name):
        off, ln = self.allocs.pop(name)
        self.retired.append((off, ln, self.resfn(name)))
        self.free.append((off, ln))
        self.free.sort()
        merged = []
        for seg in self.free:
            if merged and merged[-1][0] + merged[-1][1] == seg[0]:
                merged[-1] = (merged[-1][0], merged[-1][1] + seg[1])
            else:
                merged.append(list(seg))
        self.free = [tuple(s) for s in merged]

    def bf(self, name, cols):
        off = self.alloc(name, cols * 2, align=64)
        assert off % 2 == 0
        return self.t[:, off // 2: off // 2 + cols]

    def f32(self, name, cols):
        off = self.alloc(name, cols * 4, align=64)
        return self.t[:, off // 2: off // 2 + cols * 2].bitcast(F32)


def build_nc(upto=6, dbg=False, dbg_dram=()):
    nc = bass.Bass("TRN2", target_bir_lowering=False, debug=False)

    dt_in = [
        ("x_own", [TPC, D], F32), ("w_q", [D, D], F32), ("w_k", [D, DKV], F32),
        ("w_v", [D, DKV], F32), ("w_o", [D, D], F32), ("w_base", [D, D], F32),
        ("b_base", [1, D], F32), ("w_rt", [E, D], F32),
        ("lora_a", [E * RLO, D], F32), ("lora_b", [E, D, RLO], F32),
        ("id_bf", [128, 128], BF16), ("id_f32", [128, 128], F32),
        ("dmask", [128, 1024], BF16), ("ones_col", [128, 1], BF16),
        ("ones_row", [1, 512], BF16), ("ones_rf", [1, 128], F32),
        ("e8", [E, 128], BF16),
    ]
    I = {}
    for nm, shp, dt in dt_in:
        I[nm] = nc.dram_tensor(nm, shp, dt, kind="ExternalInput").ap()
    out = nc.dram_tensor("out", [TPC, D], F32, kind="ExternalOutput").ap()

    dram = {}
    for nm, shp, dt, kw in [
        ("xbf", [TPC, D], BF16, {}), ("qbf", [D, TPC], BF16, {}),
        ("kvin", [KVROWS, DKV], BF16, {}),
        ("kvag", [NCORES * KVROWS, DKV], BF16, {"addr_space": "Shared"}),
        ("wbase_bf", [D, D], BF16, {}), ("bstk_d", [E * RLO, D], BF16, {}),
        ("q2d", [NH, TPC], F32, {}),
        ("ztm", [TPC, D], F32, {}), ("y1f", [TPC, D], F32, {}),
        ("y1bf", [TPC, D], BF16, {}), ("fftm", [TPC, D], F32, {}),
    ]:
        dram[nm] = nc.dram_tensor(nm, shp, dt, kind="Internal", **kw).ap()
    dbg_out = {}
    if dbg:
        dbg_out["misc"] = nc.dram_tensor("dbg_misc", [128, 1024], F32,
                                         kind="ExternalOutput").ap()
    for nm in dbg_dram:
        t = dram[nm].tensor
        dbg_out[nm] = nc.dram_tensor("dbgd_" + nm, list(t.shape), t.dtype,
                                     kind="ExternalOutput").ap()
    if dbg:
        dbg_out["attn"] = nc.dram_tensor("dbg_attn", [128, NH * TPC], BF16,
                                         kind="ExternalOutput").ap()

    from contextlib import ExitStack
    with ExitStack() as ctx:
        en = ctx.enter_context
        kb = KB(nc, ctx)

        idb = en(nc.sbuf_tensor("idb", [128, 128], BF16))
        idf = en(nc.sbuf_tensor("idf", [128, 128], F32))
        maskb = en(nc.sbuf_tensor("maskb", [128, 1024], BF16))
        onec = en(nc.sbuf_tensor("onec", [128, 1], BF16))
        oner = en(nc.sbuf_tensor("oner", [1, 512], BF16))
        onerf = en(nc.sbuf_tensor("onerf", [1, 128], F32))
        e8b = en(nc.sbuf_tensor("e8b", [E, 128], BF16))
        brow = en(nc.sbuf_tensor("brow", [1, D], BF16))
        astk = en(nc.sbuf_tensor("astk", [128, D], BF16))
        wrtT = en(nc.sbuf_tensor("wrtT", [128, NC * E], BF16))
        q2st = en(nc.sbuf_tensor("q2st", [128, NH * NTT], F32))
        biasall = en(nc.sbuf_tensor("biasall", [128, NH * NTT], F32))
        epsv = en(nc.sbuf_tensor("epsv", [128, 1], F32))

        AR_COLS = 90 * 1024
        art = en(nc.sbuf_tensor("arena", [128, AR_COLS], BF16))

        PD4 = [en(nc.psum_tensor(f"pd{i}", [128, 1024], F32)) for i in range(4)]
        PS = [PD4[i // 2][:, (i % 2) * 512:(i % 2) * 512 + 512]
              for i in range(8)]
        PD = [PD4[0], PD4[1]]

        def psb(i):
            return PS[i].bitcast(BF16)

        R = {}

        def res(name):
            if name not in R:
                R[name] = Res(name)
            return R[name]

        A = Arena(art, AR_COLS * 2, res)

        def link(dsts, srcs):
            for dn in dsts:
                rd = res(dn)
                for sn in srcs:
                    rs_ = res(sn)
                    for k, v in rs_.readers.items():
                        if rd.readers.get(k, -1) < v:
                            rd.readers[k] = v
                    for k, v in rs_.writers.items():
                        if rd.writers.get(k, -1) < v:
                            rd.writers[k] = v

        p0_la = A.bf("p0_la", D)
        p0_lb = [A.bf(f"p0_lb{i}", 512) for i in range(2)]
        p0_wr = A.bf("p0_wr", D)[0:8, :]
        p0_t16 = A.bf("p0_t16", D)[0:16, :]
        p0_id8 = A.bf("p0_id8", 8)[0:8, :]

        for nm, dst in (("id_bf", idb), ("id_f32", idf), ("dmask", maskb),
                        ("ones_col", onec), ("ones_row", oner),
                        ("ones_rf", onerf), ("e8", e8b)):
            kb.dma("sync", dst[:], I[nm], writes=[res(nm)])
        kb.dma("gpsimd", brow[:], I["b_base"], writes=[res("brow")])
        kb.dma("gpsimd", p0_la[:], I["lora_a"], writes=[res("p0_la")])
        kb.dma("gpsimd", p0_wr[:], I["w_rt"], writes=[res("p0_wr")])
        kb.dma("gpsimd", dram["xbf"], I["x_own"], writes=[res("xbf")])

        kb.op("vector", lambda v: v.tensor_copy(p0_id8[:], idb[0:8, 0:8]),
              reads=[res("id_bf")], writes=[res("p0_id8")])
        kb.op("vector", lambda v: v.memset(epsv[:], EPS), writes=[res("epsv")])

        for rnd in range(2):
            for cc in range(16):
                c = rnd * 16 + cc
                kb.op("tensor", (lambda c=c, cc=cc: lambda t: t.matmul(
                    PS[0][:, cc * 8:(cc + 1) * 8],
                    p0_wr[0:8, c * 128:(c + 1) * 128], p0_id8[:],
                    start=True, stop=True))(),
                    reads=[res("p0_wr"), res("p0_id8")], writes=[res("ps0")])
            kb.op("vector", (lambda rnd=rnd: lambda v: v.tensor_copy(
                wrtT[:, rnd * 128:(rnd + 1) * 128], PS[0][:, 0:128]))(),
                reads=[res("ps0")], writes=[res("wrtT")])

        for bt in range(4):
            bank = 1 + bt % 2
            for k in range(8):
                c = bt * 8 + k
                kb.op("tensor", (lambda c=c, k=k, bank=bank: lambda t: t.transpose(
                    psb(bank)[:, k * 128:(k + 1) * 128],
                    p0_la[:, c * 128:(c + 1) * 128], idb[:]))(),
                    reads=[res("p0_la"), res("id_bf")], writes=[res(f"ps{bank}")])
            kb.op("vector", (lambda bt=bt, bank=bank: lambda v: v.tensor_copy(
                astk[:, bt * 1024:(bt + 1) * 1024], psb(bank)[:]))(),
                reads=[res(f"ps{bank}")], writes=[res("astk")])

        lbv = I["lora_b"].rearrange("e (c p) r -> e p c r", p=128)
        for e_ix in range(E):
            dstb = p0_lb[e_ix % 2]
            dv = dstb.rearrange("p (c r) -> p c r", r=RLO)
            kb.dma("gpsimd", dv, lbv[e_ix], writes=[res(f"p0_lb{e_ix % 2}")])
            for grp in range(4):
                bank = 1 + (grp % 2)
                for k in range(8):
                    c = grp * 8 + k
                    kb.op("tensor", (lambda c=c, k=k, bank=bank, dv=dv:
                                     lambda t: t.transpose(
                        psb(bank)[0:16, k * 128:(k + 1) * 128],
                        dv[:, c, :], idb[:]))(),
                        reads=[res(f"p0_lb{e_ix % 2}"), res("id_bf")],
                        writes=[res(f"ps{bank}")])
                kb.op("vector", (lambda grp=grp, bank=bank: lambda v: v.tensor_copy(
                    p0_t16[:, grp * 1024:(grp + 1) * 1024],
                    psb(bank)[0:16, 0:1024]))(),
                    reads=[res(f"ps{bank}")], writes=[res("p0_t16")])
            kb.dma("sync", dram["bstk_d"][16 * e_ix:16 * (e_ix + 1), :],
                   p0_t16[:], reads=[res("p0_t16")], writes=[res("bstk_d")])
        for nm in ("p0_la", "p0_lb0", "p0_lb1", "p0_wr", "p0_t16", "p0_id8"):
            A.release(nm)
        xfm = A.bf("xfm", NC * TPC)
        wc = [A.bf(f"wc{i}", D) for i in range(3)]
        osl = [A.bf(f"osl{i}", TPC) for i in range(2)]
        vts = A.bf("vts", TPC)
        sqb = A.bf("sqb", TPC)
        q2r = A.f32("q2r", TPC)[0:1, :]
        k2l = A.f32("k2l", 64)[0:1, 0:1]
        k2l2 = A.f32("k2l2", 64)[0:1, 0:2]
        k2lb = A.bf("k2lb", 64)[0:1, 0:8]

        for c in range(NC):
            for th in range(2):
                kb.dma("sync",
                       xfm[:, c * TPC + th * 512: c * TPC + th * 512 + 512],
                       dram["xbf"][th * 512:(th + 1) * 512,
                                   c * 128:(c + 1) * 128],
                       reads=[res("xbf")], writes=[res(f"xfm{c}")],
                       transpose=True)

        wb_slice = [0]

        def emit_wbase_slice():
            i = wb_slice[0]
            if i >= 32:
                return
            wb_slice[0] += 1
            kb.dma("gpsimd", dram["wbase_bf"][i * 128:(i + 1) * 128, :],
                   I["w_base"][i * 128:(i + 1) * 128, :],
                   writes=[res(f"wbase_bf{i // 4}")])

        def load_colblock(wsrc, j, bi):
            wv = wsrc[:, j * 128:(j + 1) * 128].rearrange("(c p) o -> p c o", p=128)
            bv = wc[bi].rearrange("p (c o) -> p c o", o=128)
            for gix in range(4):
                kb.dma("gpsimd", bv[:, gix * 8:(gix + 1) * 8, :],
                       wv[:, gix * 8:(gix + 1) * 8, :], writes=[res(f"wc{bi}")])
            emit_wbase_slice()

        def proj_mms(bi):
            for half in range(2):
                for c in range(NC):
                    kb.op("tensor", (lambda bi=bi, c=c, half=half: lambda t:
                          t.matmul(PS[half][:],
                                   wc[bi][:, c * 128:(c + 1) * 128],
                                   xfm[:, c * TPC + half * 512:
                                       c * TPC + half * 512 + 512],
                                   start=(c == 0), stop=(c == NC - 1)))(),
                        reads=[res(f"wc{bi}"), res(f"xfm{c}")],
                        writes=[res(f"ps{half}")])

        def evac_slab(bi):
            for half in range(2):
                kb.op("vector", (lambda bi=bi, half=half: lambda v:
                      v.tensor_copy(osl[bi][:, half * 512:half * 512 + 512],
                                    PS[half][:]))(),
                    reads=[res(f"ps{half}")], writes=[res(f"osl{bi}")])

        def sq_ones(slab_res, h_or_j, kind):
            kb.op("scalar", (lambda bi=h_or_j % 2: lambda sc: sc.activation(
                sqb[:], osl[bi][:], AF.Square))(),
                reads=[res(slab_res)], writes=[res("sqb")])
            for half in range(2):
                kb.op("tensor", (lambda half=half: lambda t: t.matmul(
                    PS[2 + half][0:1, :],
                    onec[:], sqb[:, half * 512:half * 512 + 512],
                    start=True, stop=True))(),
                    reads=[res("sqb"), res("ones_col")],
                    writes=[res(f"ps{2 + half}")])

        p1_pend = [None]

        def p1_flush():
            if p1_pend[0] is not None:
                fn = p1_pend[0]
                p1_pend[0] = None
                fn()

        for j in range(NKV):
            bi = j % 3
            sl = j % 2
            load_colblock(I["w_k"], j, bi)
            proj_mms(bi)
            evac_slab(sl)

            def post_k(j=j, sl=sl):
                sq_ones(f"osl{sl}", j, "k")
                for half in range(2):
                    kb.op("vector", (lambda half=half: lambda v: v.reduce_max(
                        k2l2[0:1, half:half + 1], PS[2 + half][0:1, :], AXX))(),
                        reads=[res(f"ps{2 + half}")], writes=[res("k2l2")])
                kb.op("vector", lambda v: v.reduce_max(k2l, k2l2[0:1, 0:2], AXX),
                      reads=[res("k2l2")], writes=[res("k2l")])
                kb.op("vector", (lambda j=j: lambda v: v.tensor_copy(
                    k2lb[0:1, j:j + 1], k2l))(),
                    reads=[res("k2l")], writes=[res("k2lb")])
                kb.dma("sync", dram["kvin"][j * 128:(j + 1) * 128, :],
                       osl[sl][:],
                       reads=[res(f"osl{sl}")], writes=[res("kvin")])
            p1_flush()
            p1_pend[0] = post_k

        for j in range(NKV):
            bi = j % 3
            sl = j % 2
            load_colblock(I["w_v"], j, bi)
            proj_mms(bi)
            evac_slab(sl)

            def post_v(j=j, sl=sl):
                for tt in range(NTT):
                    kb.op("tensor", (lambda sl=sl, tt=tt: lambda t: t.transpose(
                        psb(4)[:, tt * 128:(tt + 1) * 128],
                        osl[sl][:, tt * 128:(tt + 1) * 128], idb[:]))(),
                        reads=[res(f"osl{sl}"), res("id_bf")],
                        writes=[res("ps4")])
                kb.op("vector", lambda v: v.tensor_copy(vts[:], psb(4)[:]),
                      reads=[res("ps4")], writes=[res("vts")])
                kv_dst = dram["kvin"][TPC:2 * TPC,
                                      j * 128:(j + 1) * 128].rearrange(
                    "(tt p) d -> p tt d", p=128)
                kb.dma("sync", kv_dst,
                       vts.rearrange("p (tt d) -> p tt d", d=128),
                       reads=[res("vts")], writes=[res("kvin")])
            p1_flush()
            p1_pend[0] = post_v

        p1_flush()
        kb.dma("sync", dram["kvin"][2 * TPC:2 * TPC + 1, 0:8], k2lb,
               reads=[res("k2lb")], writes=[res("kvin")])

        if not SKIP_CC:
            kb.op("gpsimd", lambda g: g.collective_compute(
                "AllGather", ALU.bypass, replica_groups=[list(range(NCORES))],
                ins=[dram["kvin"].opt()], outs=[dram["kvag"].opt()]),
                reads=[res("kvin")], writes=[res("kvag")], cc=True)

        for h in range(NH):
            bi = h % 3
            sl = h % 2
            load_colblock(I["w_q"], h, bi)
            proj_mms(bi)
            evac_slab(sl)

            def post_q(h=h, sl=sl):
                sq_ones(f"osl{sl}", h, "q")
                for half in range(2):
                    kb.op("vector", (lambda half=half: lambda v: v.tensor_copy(
                        q2r[0:1, half * 512:half * 512 + 512],
                        PS[2 + half][0:1, :]))(),
                        reads=[res(f"ps{2 + half}")], writes=[res("q2r")])
                for tt in range(NTT):
                    kb.op("tensor", (lambda tt=tt: lambda t: t.transpose(
                        PS[5][:, tt:tt + 1], q2r[0:1, tt * 128:(tt + 1) * 128],
                        idf[0:1, 0:1]))(),
                        reads=[res("q2r"), res("id_f32")], writes=[res("ps5")])
                kb.op("vector", (lambda h=h: lambda v: v.tensor_copy(
                    q2st[:, h * NTT:(h + 1) * NTT], PS[5][:, 0:NTT]))(),
                    reads=[res("ps5")], writes=[res("q2st")])
                kb.dma("sync", dram["qbf"][h * 128:(h + 1) * 128, :],
                       osl[sl][:],
                       reads=[res(f"osl{sl}")], writes=[res("qbf")])
            p1_flush()
            p1_pend[0] = post_q
        p1_flush()

        if upto < 2:
            _finish_dev(nc, kb, A, out, res, dram, dbg_out)
            kb.finalize()
            return nc, dbg_out

        link(["xfm"], [f"xfm{c}" for c in range(NC)])
        for nm in ("xfm", "wc0", "wc1", "wc2", "osl0", "osl1", "vts", "sqb",
                   "q2r", "k2l", "k2l2", "k2lb"):
            A.release(nm)
        attnfm = A.bf("attnfm", NH * TPC)
        kg = [A.bf(f"kg{i}", S) for i in range(2)]
        vg = [A.bf(f"vg{i}", S) for i in range(2)]
        qh = [A.bf(f"qh{i}", TPC) for i in range(2)]
        pbuf = [A.bf(f"pbuf{i}", 1024) for i in range(12)]
        ptmp = A.bf("ptmp", 1024)
        ptb = [A.bf(f"ptb{i}", 1024) for i in range(2)]
        lparts = A.f32("lparts", 4 * 32)
        lsum4 = A.f32("lsum4", 4)
        linv4 = A.f32("linv4", 4)
        lrow = A.f32("lrow", 512)[0:1, :]
        linvb = A.f32("linvb", 512)
        oraw = [A.f32(f"oraw{i}", 512) for i in range(2)]
        k2g8 = A.bf("k2g8", 64)[0:8, 0:8]
        k2m8 = A.f32("k2m8", 8)[0:8, 0:1]
        k2rowS = A.f32("k2rowS", 8)[0:1, :]
        k2mb = A.f32("k2mb", 8)
        tscr = A.f32("tscr", NH * NTT)
        tscr2 = A.f32("tscr2", NH * NTT)

        link(["pd0"], ["ps0", "ps1"])
        link(["pd1"], ["ps2", "ps3"])
        k2src = bass.AP(dram["kvag"].tensor, 2 * TPC * DKV,
                        [[1, 8], [KVROWS * DKV, 8]])
        kb.dma("sync", k2g8, k2src, reads=[res("kvag")], writes=[res("k2g8")],
               allow_slow_non_contiguous=True)
        kb.op("vector", lambda v: v.reduce_max(k2m8, k2g8, AXX),
              reads=[res("k2g8")], writes=[res("k2m8")])
        kb.op("tensor", lambda t: t.transpose(PS[6][0:1, 0:8], k2m8,
                                              idf[0:8, 0:8]),
              reads=[res("k2m8"), res("id_f32")], writes=[res("ps6")])
        kb.op("vector", lambda v: v.tensor_copy(k2rowS, PS[6][0:1, 0:8]),
              reads=[res("ps6")], writes=[res("k2rowS")])
        for g in range(8):
            kb.op("tensor", (lambda g=g: lambda t: t.matmul(
                PS[6][:, g:g + 1], onerf[:], k2rowS[0:1, g:g + 1],
                start=True, stop=True))(),
                reads=[res("k2rowS"), res("ones_rf")], writes=[res("ps6")])
        kb.op("vector", lambda v: v.tensor_copy(k2mb[:], PS[6][:, 0:8]),
              reads=[res("ps6")], writes=[res("k2mb")])
        for g in range(8):
            sl = slice(g * 32, (g + 1) * 32)
            kb.op("vector", (lambda g=g, sl=sl: lambda v: v.tensor_scalar_mul(
                tscr[:, sl], q2st[:, sl], k2mb[:, g:g + 1]))(),
                reads=[res("q2st"), res("k2mb")], writes=[res("tscr")])
        kb.op("scalar", lambda sc: sc.activation(
            tscr2[:], tscr[:], AF.Ln, scale=SCALE * SCALE),
            reads=[res("tscr")], writes=[res("tscr2")])
        kb.op("scalar", lambda sc: sc.activation(
            tscr[:], tscr2[:], AF.Exp, scale=0.5),
            reads=[res("tscr2")], writes=[res("tscr")])
        kb.op("vector", lambda v: v.tensor_scalar(
            biasall[:], tscr[:], -1.0, -1.0, op0=ALU.mult, op1=ALU.add),
            reads=[res("tscr")], writes=[res("biasall")])

        def load_kv(g, par):
            kgv = kg[par].rearrange("p (j m) -> p j m", j=8)
            for j8 in range(8):
                src = bass.AP(dram["kvag"].tensor,
                              (j8 * KVROWS + g * 128) * DKV,
                              [[DKV, 128], [1, TPC]])
                kb.dma("gpsimd", kgv[:, j8, :], src,
                       reads=[res("kvag")], writes=[res(f"kg{par}")])
            for j8 in range(8):
                src = bass.AP(dram["kvag"].tensor,
                              ((j8 * KVROWS + TPC) * DKV) + g * 128,
                              [[DKV, 128], [128 * DKV, 8], [1, 128]])
                dstv = vg[par].rearrange("p (l j d) -> p l j d", l=8, j=8)
                kb.dma("gpsimd", dstv[:, :, j8, :], src,
                       reads=[res("kvag")], writes=[res(f"vg{par}")])

        load_kv(0, 0)
        nheads_bias = res("biasall")

        PTB_B = (4, 5)
        s_ctr = [0]
        pt_ctr = [0]

        for g in range(8):
            par = g % 2
            if g < 7:
                load_kv(g + 1, (g + 1) % 2)
            kgv = kg[par].rearrange("p (j m) -> p j m", j=8)
            for hl in range(4):
                h = 4 * g + hl
                qp = h % 2
                kb.dma("sync", qh[qp][:, :], dram["qbf"][h * 128:(h + 1) * 128, :],
                       reads=[res("qbf")], writes=[res(f"qh{qp}")])
                for m in range(2):
                    tt0 = 4 * m
                    tts = list(range(tt0, tt0 + 4))
                    first_pv = [True]
                    pend = [None]

                    def emit_A_tt(L, tt):
                        sp = s_ctr[0] % 2
                        s_ctr[0] += 1
                        for half in range(2):
                            kb.op("tensor", (lambda sp=sp, tt=tt, L=L,
                                             half=half, qp=qp, kgv=kgv:
                                  lambda t: t.matmul(
                                      PD[sp][:, half * 512:half * 512 + 512],
                                      qh[qp][:, tt * 128:(tt + 1) * 128],
                                      kgv[:, half * 4:half * 4 + 4,
                                          L * 128:(L + 1) * 128],
                                      start=True, stop=True))(),
                                reads=[res(f"qh{qp}"), res(f"kg{par}")],
                                writes=[res(f"pd{sp}")])
                        dl = (tt - tt0) * 3 + (L % 3)
                        bi = L
                        bcol = biasall[:, h * 8 + tt:h * 8 + tt + 1]
                        if L < tt:
                            kb.op("scalar", (lambda sp=sp, dl=dl, bcol=bcol,
                                             tt=tt, bi=bi:
                                  lambda sc: sc.activation(
                                      pbuf[dl][:],
                                      PD[sp][:], AF.Exp, bias=bcol,
                                      scale=SCALE,
                                      accum_out=lparts[:, (tt - tt0) * 32
                                                       + bi:
                                                       (tt - tt0) * 32
                                                       + bi + 1]))(),
                                reads=[res(f"pd{sp}"), nheads_bias],
                                writes=[res(f"pbuf{dl}"), res("lparts")])
                        else:
                            kb.op("scalar", (lambda sp=sp, bcol=bcol:
                                  lambda sc: sc.activation(
                                      ptmp[:], PD[sp][:], AF.Exp,
                                      bias=bcol, scale=SCALE))(),
                                reads=[res(f"pd{sp}"), nheads_bias],
                                writes=[res("ptmp")])
                            kb.op("vector", (lambda dl=dl, tt=tt, bi=bi:
                                  lambda v: v.scalar_tensor_tensor(
                                      pbuf[dl][:],
                                      ptmp[:], 1.0, maskb[:],
                                      op0=ALU.mult, op1=ALU.mult,
                                      accum_out=lparts[:,
                                          (tt - tt0) * 32 + bi:
                                          (tt - tt0) * 32 + bi + 1]))(),
                                reads=[res("ptmp"), res("dmask")],
                                writes=[res(f"pbuf{dl}"), res("lparts")])

                    def flush_pend():
                        if not pend[0]:
                            return
                        (ptp, tbp, pvs) = pend[0]
                        pend[0] = None
                        kb.op("vector", (lambda ptp=ptp, tbp=tbp: lambda v:
                              v.tensor_copy(ptb[tbp][:], psb(ptp)[:]))(),
                            reads=[res(f"ps{ptp}")],
                            writes=[res(f"ptb{tbp}")])
                        for (chi, jh, col0, ncols, st, pv_par) in pvs:
                            kb.op("tensor", (lambda chi=chi, tbp=tbp, jh=jh,
                                             col0=col0, ncols=ncols, st=st,
                                             pv_par=pv_par: lambda t: t.matmul(
                                PS[6][:, col0:col0 + ncols],
                                vg[pv_par][:, chi * 128:(chi + 1) * 128],
                                ptb[tbp][:, jh * 512:jh * 512 + ncols],
                                start=st, stop=False,
                                skip_group_check=True))(),
                                reads=[res(f"ptb{tbp}"), res(f"vg{pv_par}")],
                                writes=[res("ps6")])

                    def emit_B_j8p(L, j8p):
                        vtts = [tt for tt in tts if tt >= L]
                        if not vtts:
                            return
                        ncols = len(vtts) * 128
                        col0 = (vtts[0] - tt0) * 128
                        ptp = PTB_B[pt_ctr[0] % 2]
                        tbp = pt_ctr[0] % 2
                        pt_ctr[0] += 1
                        for jh in range(2):
                            j8 = 2 * j8p + jh
                            for ix, tt in enumerate(vtts):
                                dl = (tt - tt0) * 3 + (L % 3)
                                kb.op("tensor", (lambda ptp=ptp, ix=ix,
                                                 dl=dl, j8=j8, jh=jh:
                                      lambda t: t.transpose(
                                    psb(ptp)[:, jh * 512 + ix * 128:
                                             jh * 512 + (ix + 1) * 128],
                                    pbuf[dl][:, j8 * 128:(j8 + 1) * 128],
                                    idb[:]))(),
                                    reads=[res(f"pbuf{dl}"), res("id_bf")],
                                    writes=[res(f"ps{ptp}")])
                        pvs = []
                        for jh in range(2):
                            j8 = 2 * j8p + jh
                            chi = L * 8 + j8
                            st = first_pv[0]
                            first_pv[0] = False
                            pvs.append((chi, jh, col0, ncols, st, par))
                        prev = pend[0]
                        pend[0] = (ptp, tbp, pvs)
                        if prev is not None:
                            tmp = pend[0]
                            pend[0] = prev
                            flush_pend()
                            pend[0] = tmp

                    Lmax = tt0 + 3
                    for tt in tts:
                        emit_A_tt(0, tt)
                    for tt in tts:
                        if tt >= 1:
                            emit_A_tt(1, tt)
                    for L in range(2, Lmax + 1):
                        amini = [tt for tt in tts if tt >= L]
                        for i in range(4):
                            if i < len(amini):
                                emit_A_tt(L, amini[i])
                            emit_B_j8p(L - 2, i)
                    emit_B_j8p(Lmax - 1, 0)
                    emit_B_j8p(Lmax - 1, 1)
                    emit_B_j8p(Lmax - 1, 2)
                    emit_B_j8p(Lmax - 1, 3)
                    for i in range(4):
                        emit_B_j8p(Lmax, i)
                    flush_pend()
                    orp = (h * 2 + m) % 2
                    kb.op("vector", (lambda orp=orp: lambda v: v.tensor_copy(
                        oraw[orp][:], PS[6][:]))(),
                        reads=[res("ps6")], writes=[res(f"oraw{orp}")])
                    for ix, tt in enumerate(tts):
                        kb.op("vector", (lambda ix=ix, nbi=tt + 1:
                              lambda v: v.reduce_sum(
                                  lsum4[:, ix:ix + 1],
                                  lparts[:, ix * 32:ix * 32 + nbi],
                                  AXX))(),
                            reads=[res("lparts")], writes=[res("lsum4")])
                    kb.op("vector", lambda v: v.reciprocal(
                        linv4[:], lsum4[:]),
                        reads=[res("lsum4")], writes=[res("linv4")])
                    for ix in range(4):
                        kb.op("tensor", (lambda ix=ix: lambda t: t.transpose(
                            PS[7][0:1, ix * 128:(ix + 1) * 128],
                            linv4[:, ix:ix + 1], idf[:]))(),
                            reads=[res("linv4"), res("id_f32")],
                            writes=[res("ps7")])
                    kb.op("vector", lambda v: v.tensor_copy(lrow, PS[7][0:1, :]),
                          reads=[res("ps7")], writes=[res("lrow")])
                    kb.op("tensor", lambda t: t.matmul(
                        PS[7][:], onerf[:], lrow, start=True, stop=True),
                        reads=[res("lrow"), res("ones_rf")], writes=[res("ps7")])
                    kb.op("vector", lambda v: v.tensor_copy(linvb[:], PS[7][:]),
                          reads=[res("ps7")], writes=[res("linvb")])
                    acol = h * TPC + m * 512
                    kb.op("vector", (lambda acol=acol, orp=orp: lambda v:
                          v.tensor_tensor(attnfm[:, acol:acol + 512],
                                          oraw[orp][:], linvb[:],
                                          op=ALU.mult))(),
                        reads=[res(f"oraw{orp}"), res("linvb")],
                        writes=[res("attnfm")])
        if dbg:
            for hc in range(NH):
                kb.dma("sync", dbg_out["attn"][:, hc * TPC:(hc + 1) * TPC],
                       attnfm[:, hc * TPC:(hc + 1) * TPC],
                       reads=[res("attnfm")], writes=[res("outw")])
            kb.dma("sync", dbg_out["misc"][:, 0:256], biasall[:],
                   reads=[res("biasall")], writes=[res("outw")])
            kb.dma("sync", dbg_out["misc"][:, 256:512], q2st[:],
                   reads=[res("q2st")], writes=[res("outw")])
            kb.dma("sync", dbg_out["misc"][:, 512:520], k2mb[:],
                   reads=[res("k2mb")], writes=[res("outw")])
            kb.dma("sync", dbg_out["misc"][0:8, 520:524],
                   k2g8.bitcast(F32)[0:8, 0:4],
                   reads=[res("k2g8")], writes=[res("outw")])

        if upto < 3:
            _finish_dev(nc, kb, A, out, res, dram, dbg_out)
            kb.finalize()
            return nc, dbg_out

        for nm in ("kg0", "kg1", "vg0", "vg1", "qh0", "qh1", "ptmp",
                   "ptb0", "ptb1", "lparts", "lsum4", "linv4", "lrow", "linvb",
                   "oraw0", "oraw1",
                   "k2g8", "k2m8", "k2rowS", "k2mb", "tscr", "tscr2"):
            A.release(nm)
        for i in range(12):
            A.release(f"pbuf{i}")

        link(["ps0", "ps1"], ["pd0"])
        link(["ps2", "ps3"], ["pd1"])
        woc = [A.bf(f"woc{i}", D) for i in range(2)]
        tmpo = [A.f32(f"tmpo{i}", TPC) for i in range(2)]
        tmpz = [A.f32(f"tmpz{i}", TPC) for i in range(2)]

        def fm_to_tm_phase(rhs_sb, rhs_res, lhs_loader, dst_dram, dst_res,
                           extra_mms=None):
            def emit_tail(oc):
                zi = oc % 2
                for tt in range(NTT):
                    pb_ = 2 + tt % 2
                    kb.op("tensor", (lambda pb_=pb_, tt=tt, zi=zi: lambda t:
                          t.transpose(PS[pb_][:, 0:128],
                                      tmpo[zi][:, tt * 128:(tt + 1) * 128],
                                      idf[:]))(),
                        reads=[res(f"tmpo{zi}"), res("id_f32")],
                        writes=[res(f"ps{pb_}")])
                    kb.op("scalar", (lambda pb_=pb_, tt=tt, zi=zi: lambda sc:
                          sc.copy(tmpz[zi][:, tt * 128:(tt + 1) * 128],
                                  PS[pb_][:, 0:128]))(),
                        reads=[res(f"ps{pb_}")], writes=[res(f"tmpz{zi}")])
                dstv = dst_dram.rearrange("(tt p) d -> p tt d", p=128)
                for th in range(2):
                    kb.dma("sync", dstv[:, th * 4:(th + 1) * 4,
                                        oc * 128:(oc + 1) * 128],
                           tmpz[zi][:, th * 512:(th + 1) * 512].rearrange(
                               "p (tt o) -> p tt o", o=128),
                           reads=[res(f"tmpz{zi}")],
                           writes=[res(dst_res + str(th))])

            for oc in range(NC):
                bi = oc % 2
                lhs_loader(oc, bi)
                for half in range(2):
                    for c in range(NC):
                        kb.op("tensor", (lambda bi=bi, c=c, half=half:
                              lambda t: t.matmul(
                                  PS[half][:],
                                  woc[bi][:, c * 128:(c + 1) * 128],
                                  rhs_sb[:, c * TPC + half * 512:
                                         c * TPC + half * 512 + 512],
                                  start=(c == 0),
                                  stop=(c == NC - 1 and extra_mms is None)))(),
                            reads=[res(f"woc{bi}"), rhs_res],
                            writes=[res(f"ps{half}")])
                    if extra_mms is not None:
                        extra_mms(oc, half)
                    zi0 = oc % 2
                    kb.op("vector", (lambda half=half, zi=zi0: lambda v:
                          v.tensor_copy(tmpo[zi][:, half * 512:half * 512 + 512],
                                        PS[half][:]))(),
                        reads=[res(f"ps{half}")], writes=[res(f"tmpo{zi0}")])
                if oc > 0:
                    emit_tail(oc - 1)
            emit_tail(NC - 1)

        def wo_loader(oc, bi):
            wv = I["w_o"][:, oc * 128:(oc + 1) * 128].rearrange(
                "(c p) o -> p c o", p=128)
            bv = woc[bi].rearrange("p (c o) -> p c o", o=128)
            for gix in range(4):
                kb.dma("gpsimd", bv[:, gix * 8:(gix + 1) * 8, :],
                       wv[:, gix * 8:(gix + 1) * 8, :], writes=[res(f"woc{bi}")])

        fm_to_tm_phase(attnfm, res("attnfm"), wo_loader, dram["ztm"], "ztm")

        if upto < 4:
            _finish_dev(nc, kb, A, out, res, dram, dbg_out)
            kb.finalize()
            return nc, dbg_out

        A.release("attnfm")
        A.release("woc0")
        A.release("woc1")
        A.release("tmpo0")
        A.release("tmpo1")
        A.release("tmpz0")
        A.release("tmpz1")

        def ln_phase(src_dram, src_res_pfx, res_dram, store_fn):
            bufa = [A.f32(f"lnA{i}", D) for i in range(3)]
            bufb = [A.f32(f"lnB{i}", D) for i in range(3)]
            bufz = [A.f32(f"lnZ{i}", D) for i in range(3)]
            svecs = [A.f32(f"lnS{i}", 16) for i in range(3)]
            for tt in range(NTT):
                p_ = tt % 3
                svec = svecs[p_]
                rz = res(f"lnZ{p_}")
                ra = res(f"lnA{p_}")
                rb = res(f"lnB{p_}")
                rs = res(f"lnS{p_}")
                kb.dma("sync", bufa[p_][:], src_dram[tt * 128:(tt + 1) * 128, :],
                       reads=[res(src_res_pfx + str(tt // 4))], writes=[ra])
                kb.dma("sync", bufb[p_][:],
                       res_dram[tt * 128:(tt + 1) * 128, :],
                       reads=[res("y1f")] if res_dram is dram["y1f"] else [],
                       writes=[rb])
                kb.op("vector", (lambda p_=p_: lambda v: v.tensor_add(
                    bufz[p_][:], bufa[p_][:], bufb[p_][:]))(),
                    reads=[ra, rb], writes=[rz])
                kb.op("vector", (lambda p_=p_: lambda v: v.reduce_sum(
                    svec[:, 0:1], bufz[p_][:], AXX))(),
                    reads=[rz], writes=[rs])
                kb.op("scalar", (lambda p_=p_: lambda sc: sc.activation(
                    bufa[p_][:], bufz[p_][:], AF.Square))(),
                    reads=[rz], writes=[ra])
                kb.op("vector", (lambda p_=p_: lambda v: v.reduce_sum(
                    svec[:, 1:2], bufa[p_][:], AXX))(),
                    reads=[ra], writes=[rs])
                kb.op("vector", lambda v: v.tensor_scalar_mul(
                    svec[:, 2:3], svec[:, 0:1], 1.0 / D),
                    reads=[rs], writes=[rs])
                kb.op("scalar", lambda sc: sc.activation(
                    svec[:, 3:4], svec[:, 2:3], AF.Square),
                    reads=[rs], writes=[rs])
                kb.op("vector", lambda v: v.tensor_scalar(
                    svec[:, 4:5], svec[:, 1:2], 1.0 / D, None, op0=ALU.mult),
                    reads=[rs], writes=[rs])
                kb.op("vector", lambda v: v.tensor_sub(
                    svec[:, 5:6], svec[:, 4:5], svec[:, 3:4]),
                    reads=[rs], writes=[rs])
                kb.op("scalar", lambda sc: sc.activation(
                    svec[:, 6:7], svec[:, 5:6], AF.Ln, bias=epsv[:]),
                    reads=[rs, res("epsv")], writes=[rs])
                kb.op("scalar", lambda sc: sc.activation(
                    svec[:, 7:8], svec[:, 6:7], AF.Exp, scale=-0.5),
                    reads=[rs], writes=[rs])
                kb.op("vector", lambda v: v.tensor_mul(
                    svec[:, 8:9], svec[:, 2:3], svec[:, 7:8]),
                    reads=[rs], writes=[rs])
                kb.op("vector", (lambda p_=p_: lambda v: v.tensor_scalar(
                    bufb[p_][:], bufz[p_][:], svec[:, 2:3], svec[:, 7:8],
                    op0=ALU.subtract, op1=ALU.mult))(),
                    reads=[rz, rs], writes=[rb])
                store_fn(tt, bufb[p_], rb)
            for i in range(3):
                A.release(f"lnA{i}")
                A.release(f"lnB{i}")
                A.release(f"lnZ{i}")
                A.release(f"lnS{i}")

        def y1_store(tt, buf, rbuf):
            kb.dma("scalar", dram["y1f"][tt * 128:(tt + 1) * 128, :], buf[:],
                   reads=[rbuf], writes=[res("y1f")])
            kb.dma("gpsimd", dram["y1bf"][tt * 128:(tt + 1) * 128, :], buf[:],
                   reads=[rbuf], writes=[res(f"y1bf{tt // 4}")])

        ln_phase(dram["ztm"], "ztm", I["x_own"], y1_store)

        if upto < 5:
            _finish_dev(nc, kb, A, out, res, dram, dbg_out)
            kb.finalize()
            return nc, dbg_out

        yfm = A.bf("yfm", NC * TPC)
        wbt = [A.bf(f"wbt{i}", NC * 512) for i in range(2)]
        bstk = A.bf("bstk", D)
        logf = A.f32("logf", TPC)[0:8, :]
        ltm = A.f32("ltm", 8)
        e8t = A.f32("e8t", 8)
        cnt = A.f32("cnt", 8)
        tj8 = A.f32("tj8", 8)
        wmsk = A.f32("wmsk", 8)
        wt8 = A.f32("wt8", 8)
        sc8 = A.f32("sc8", 8)
        wfmb = A.bf("wfmb", TPC)[0:8, :]
        wbs = A.f32("wbs", TPC)
        hs = A.f32("hs", TPC)
        gsb = A.bf("gsb", TPC)
        tmpo5 = [A.f32(f"tmpo{i}", TPC) for i in range(2)]
        tmpz5 = [A.f32(f"tmpz{i}", TPC) for i in range(2)]

        kb.dma("sync", bstk[:], dram["bstk_d"],
               reads=[res("bstk_d")], writes=[res("bstk")])
        for c in range(NC):
            for th in range(2):
                kb.dma("sync",
                       yfm[:, c * TPC + th * 512: c * TPC + th * 512 + 512],
                       dram["y1bf"][th * 512:(th + 1) * 512,
                                    c * 128:(c + 1) * 128],
                       reads=[res(f"y1bf{th}")], writes=[res("yfm")],
                       transpose=True)

        for half in range(2):
            for c in range(NC):
                kb.op("tensor", (lambda c=c, half=half: lambda t: t.matmul(
                    PS[0][0:8, :], wrtT[:, c * 8:(c + 1) * 8],
                    yfm[:, c * TPC + half * 512: c * TPC + half * 512 + 512],
                    start=(c == 0), stop=(c == NC - 1)))(),
                    reads=[res("wrtT"), res("yfm")], writes=[res("ps0")])
            kb.op("vector", (lambda half=half: lambda v: v.tensor_copy(
                logf[:, half * 512:half * 512 + 512], PS[0][0:8, :]))(),
                reads=[res("ps0")], writes=[res("logf")])

        for tt in range(NTT):
            kb.op("tensor", (lambda tt=tt: lambda t: t.matmul(
                PS[1][:, 0:8], logf[0:8, tt * 128:(tt + 1) * 128],
                idf[0:8, 0:8], start=True, stop=True))(),
                reads=[res("logf"), res("id_f32")], writes=[res("ps1")])
            kb.op("vector", lambda v: v.tensor_copy(ltm[:], PS[1][:, 0:8]),
                  reads=[res("ps1")], writes=[res("ltm")])
            kb.op("vector", lambda v: v.reduce_max(sc8[:, 0:1], ltm[:], AXX),
                  reads=[res("ltm")], writes=[res("sc8")])
            kb.op("vector", lambda v: v.tensor_scalar_mul(
                sc8[:, 1:2], sc8[:, 0:1], -1.0),
                reads=[res("sc8")], writes=[res("sc8")])
            kb.op("scalar", lambda sc: sc.activation(
                e8t[:], ltm[:], AF.Exp, bias=sc8[:, 1:2]),
                reads=[res("ltm"), res("sc8")], writes=[res("e8t")])
            kb.op("vector", lambda v: v.memset(cnt[:], 0.0),
                  writes=[res("cnt")])
            for j in range(E):
                kb.op("vector", (lambda j=j: lambda v: v.tensor_single_scalar(
                    tj8[:], e8t[:], e8t[:, j:j + 1], op=ALU.is_lt))(),
                    reads=[res("e8t")], writes=[res("tj8")])
                kb.op("vector", lambda v: v.tensor_add(cnt[:], cnt[:], tj8[:]),
                      reads=[res("tj8"), res("cnt")], writes=[res("cnt")])
            kb.op("vector", lambda v: v.tensor_single_scalar(
                wmsk[:], cnt[:], 2.0, op=ALU.is_lt),
                reads=[res("cnt")], writes=[res("wmsk")])
            kb.op("vector", lambda v: v.tensor_mul(wt8[:], e8t[:], wmsk[:]),
                  reads=[res("e8t"), res("wmsk")], writes=[res("wt8")])
            kb.op("vector", lambda v: v.reduce_sum(sc8[:, 2:3], wt8[:], AXX),
                  reads=[res("wt8")], writes=[res("sc8")])
            kb.op("vector", lambda v: v.reciprocal(sc8[:, 3:4], sc8[:, 2:3]),
                  reads=[res("sc8")], writes=[res("sc8")])
            kb.op("vector", lambda v: v.tensor_scalar_mul(
                wt8[:], wt8[:], sc8[:, 3:4]),
                reads=[res("wt8"), res("sc8")], writes=[res("wt8")])
            kb.op("tensor", lambda t: t.transpose(PS[1][0:8, 0:128], wt8[:],
                                                  idf[:]),
                  reads=[res("wt8"), res("id_f32")], writes=[res("ps1")])
            kb.op("vector", (lambda tt=tt: lambda v: v.tensor_copy(
                wfmb[:, tt * 128:(tt + 1) * 128], PS[1][0:8, 0:128]))(),
                reads=[res("ps1")], writes=[res("wfmb")])

        for half in range(2):
            kb.op("tensor", (lambda half=half: lambda t: t.matmul(
                PS[2][:], e8b[:], wfmb[:, half * 512:half * 512 + 512],
                start=True, stop=True))(),
                reads=[res("wfmb"), res("e8")], writes=[res("ps2")])
            kb.op("vector", (lambda half=half: lambda v: v.tensor_copy(
                wbs[:, half * 512:half * 512 + 512], PS[2][:]))(),
                reads=[res("ps2")], writes=[res("wbs")])
            for c in range(NC):
                kb.op("tensor", (lambda c=c, half=half: lambda t: t.matmul(
                    PS[3][:], astk[:, c * 128:(c + 1) * 128],
                    yfm[:, c * TPC + half * 512: c * TPC + half * 512 + 512],
                    start=(c == 0), stop=(c == NC - 1)))(),
                    reads=[res("astk"), res("yfm")], writes=[res("ps3")])
            kb.op("vector", (lambda half=half: lambda v: v.tensor_copy(
                hs[:, half * 512:half * 512 + 512], PS[3][:]))(),
                reads=[res("ps3")], writes=[res("hs")])
        kb.op("vector", lambda v: v.tensor_mul(gsb[:], hs[:], wbs[:]),
              reads=[res("hs"), res("wbs")], writes=[res("gsb")])

        def load_wbt(ob):
            par = ob % 2
            for c in range(NC):
                kb.dma("sync", wbt[par][:, c * 512:(c + 1) * 512],
                       dram["wbase_bf"][ob * 512:(ob + 1) * 512,
                                        c * 128:(c + 1) * 128],
                       reads=[res(f"wbase_bf{ob}")],
                       writes=[res(f"wbt{par}")], transpose=True)

        load_wbt(0)
        load_wbt(1)

        def wb_loader(oc, bi):
            pass

        def wb_lhs(oc):
            ob = oc // 4
            par = ob % 2
            oc4 = oc % 4
            return wbt[par], par, oc4

        def p5_loader(oc, bi):
            wb_loader(oc, bi)

        def p5_extra(oc, half):
            kb.op("tensor", (lambda oc=oc, half=half: lambda t: t.matmul(
                PS[half][:], bstk[:, oc * 128:(oc + 1) * 128],
                gsb[:, half * 512:half * 512 + 512],
                start=False, stop=False, skip_group_check=True))(),
                reads=[res("bstk"), res("gsb")], writes=[res(f"ps{half}")])
            kb.op("tensor", (lambda oc=oc, half=half: lambda t: t.matmul(
                PS[half][:], brow[0:1, oc * 128:(oc + 1) * 128],
                oner[:], start=False, stop=True, skip_group_check=True))(),
                reads=[res("brow"), res("ones_row")], writes=[res(f"ps{half}")])

        def p5_tail(oc):
            zi = oc % 2
            for tt in range(NTT):
                pb_ = 2 + tt % 2
                kb.op("tensor", (lambda pb_=pb_, tt=tt, zi=zi: lambda t:
                      t.transpose(PS[pb_][:, 0:128],
                                  tmpo5[zi][:, tt * 128:(tt + 1) * 128],
                                  idf[:]))(),
                    reads=[res(f"tmpo{zi}"), res("id_f32")],
                    writes=[res(f"ps{pb_}")])
                kb.op("scalar", (lambda pb_=pb_, tt=tt, zi=zi: lambda sc:
                      sc.copy(tmpz5[zi][:, tt * 128:(tt + 1) * 128],
                              PS[pb_][:, 0:128]))(),
                    reads=[res(f"ps{pb_}")], writes=[res(f"tmpz{zi}")])
            dstv = dram["fftm"].rearrange("(tt p) d -> p tt d", p=128)
            for th in range(2):
                kb.dma("sync", dstv[:, th * 4:(th + 1) * 4,
                                    oc * 128:(oc + 1) * 128],
                       tmpz5[zi][:, th * 512:(th + 1) * 512].rearrange(
                           "p (tt o) -> p tt o", o=128),
                       reads=[res(f"tmpz{zi}")], writes=[res(f"fftm{th}")])

        for oc in range(NC):
            p5_loader(oc, oc % 2)
            slab, par, oc4 = wb_lhs(oc)
            for half in range(2):
                for c in range(NC):
                    kb.op("tensor", (lambda slab=slab, par=par, oc4=oc4, c=c,
                                     half=half: lambda t: t.matmul(
                        PS[half][:],
                        slab[:, c * 512 + oc4 * 128: c * 512 + oc4 * 128 + 128],
                        yfm[:, c * TPC + half * 512: c * TPC + half * 512 + 512],
                        start=(c == 0), stop=False,
                        skip_group_check=True))(),
                        reads=[res(f"wbt{par}"), res("yfm")],
                        writes=[res(f"ps{half}")])
                p5_extra(oc, half)
                zi0 = oc % 2
                kb.op("vector", (lambda half=half, zi=zi0: lambda v:
                      v.tensor_copy(tmpo5[zi][:, half * 512:half * 512 + 512],
                                    PS[half][:]))(),
                    reads=[res(f"ps{half}")], writes=[res(f"tmpo{zi0}")])
            if oc > 0:
                p5_tail(oc - 1)
            if oc % 4 == 3 and oc // 4 + 2 < 8:
                load_wbt(oc // 4 + 2)
        p5_tail(NC - 1)

        if upto < 6:
            _finish_dev(nc, kb, A, out, res, dram, dbg_out)
            kb.finalize()
            return nc, dbg_out

        for nm in ("yfm", "wbt0", "wbt1", "bstk", "logf", "ltm", "e8t", "cnt",
                   "tj8", "wmsk", "wt8", "sc8", "wfmb", "wbs", "hs", "gsb",
                   "tmpo0", "tmpo1", "tmpz0", "tmpz1"):
            A.release(nm)

        def out_store(tt, buf, rbuf):
            kb.dma("scalar", out[tt * 128:(tt + 1) * 128, :], buf[:],
                   reads=[rbuf], writes=[res("outw")])

        ln_phase(dram["fftm"], "fftm", dram["y1f"], out_store)
        kb.op("sync", lambda e: e.nop(), reads=[res("outw")])

        kb.finalize()
    return nc, dbg_out


def _finish_dev(nc, kb, A, out, res, dram=None, dbg_map=None):
    zz = A.f32("zz_dev", 1024)
    kb.op("gpsimd", lambda g: g.memset(zz[:], 0.0), writes=[res("zz_dev")])
    for tt in range(NTT):
        for q4 in range(4):
            kb.dma("gpsimd", out[tt * 128:(tt + 1) * 128,
                                 q4 * 1024:(q4 + 1) * 1024], zz[:],
                   reads=[res("zz_dev")], writes=[res("outw")])
    if dbg_map:
        for nm, dst in dbg_map.items():
            if nm not in dram:
                continue
            kb.dma("sync", dst, dram[nm], reads=[res(nm)], writes=[res("outw")])
    kb.op("gpsimd", lambda g: g.engine_nop(), reads=[res("outw")])


def _host_consts(core_i):
    m = np.zeros((128, 1024), dtype=np.float32)
    for j8 in range(8):
        k = 0 if j8 <= core_i else -1
        m[:, j8 * 128:(j8 + 1) * 128] = np.tril(np.ones((128, 128), np.float32), k)
    e8v = np.zeros((E, 128), dtype=np.float32)
    for e_ix in range(E):
        e8v[e_ix, e_ix * RLO:(e_ix + 1) * RLO] = MOE_SCALING
    return {
        "id_bf": np.eye(128, dtype=ml_dtypes.bfloat16),
        "id_f32": np.eye(128, dtype=np.float32),
        "dmask": m.astype(ml_dtypes.bfloat16),
        "ones_col": np.ones((128, 1), dtype=ml_dtypes.bfloat16),
        "ones_row": np.ones((1, 512), dtype=ml_dtypes.bfloat16),
        "ones_rf": np.ones((1, 128), dtype=np.float32),
        "e8": e8v.astype(ml_dtypes.bfloat16),
    }


def make_in_maps(x, wq, wk, wv, wo, wbase, bbase, wrt, la, lbv):
    shared = {
        "w_q": np.ascontiguousarray(wq, dtype=np.float32),
        "w_k": np.ascontiguousarray(wk, dtype=np.float32),
        "w_v": np.ascontiguousarray(wv, dtype=np.float32),
        "w_o": np.ascontiguousarray(wo, dtype=np.float32),
        "w_base": np.ascontiguousarray(wbase, dtype=np.float32),
        "b_base": np.ascontiguousarray(bbase, dtype=np.float32).reshape(1, D),
        "w_rt": np.ascontiguousarray(wrt, dtype=np.float32),
        "lora_a": np.ascontiguousarray(la, dtype=np.float32).reshape(E * RLO, D),
        "lora_b": np.ascontiguousarray(lbv, dtype=np.float32),
    }
    in_maps = []
    for i in range(NCORES):
        mm = dict(shared)
        mm["x_own"] = np.ascontiguousarray(x[i::8, :], dtype=np.float32)
        mm.update(_host_consts(i))
        in_maps.append(mm)
    return in_maps


def kernel(L0_attn_hidden_states, L0_attn_w_q, L0_attn_w_k, L0_attn_w_v,
           L0_attn_w_o, L0_ffn_W_base, L0_ffn_b_base, L0_ffn_W_router,
           L0_ffn_lora_A, L0_ffn_lora_B):
    x = np.asarray(L0_attn_hidden_states).reshape(S, D)
    nc, _ = build_nc()
    in_maps = make_in_maps(x, L0_attn_w_q, L0_attn_w_k, L0_attn_w_v,
                           L0_attn_w_o, L0_ffn_W_base, L0_ffn_b_base,
                           L0_ffn_W_router, L0_ffn_lora_A, L0_ffn_lora_B)
    res_ = run_bass_kernel_spmd(nc, in_maps, core_ids=list(range(NCORES)))
    full = np.empty((S, D), dtype=np.float32)
    for i in range(NCORES):
        full[i::8, :] = res_.results[i]["out"]
    return full.reshape(1, S, D)
